# version 17
# baseline (speedup 1.0000x reference)
import numpy as np
import ml_dtypes
import concourse.bass as bass
import concourse.mybir as mybir
from concourse.bass_utils import run_bass_kernel_spmd
from contextlib import ExitStack

F32 = mybir.dt.float32
BF16 = mybir.dt.bfloat16
U8 = mybir.dt.uint8
AF = mybir.ActivationFunctionType
ALU = mybir.AluOpType
AX = mybir.AxisListType.X

D = 2048
S = 4096
HD = 8192
TT = 512
NTT = S // TT
OWN = 1024
EPS = 1e-5
FWHT_SCALE = 1.0 / float(np.sqrt(HD))

ENGINES = ("tensor", "vector", "scalar", "gpsimd", "sync")


class Plan:

    def __init__(self, nc):
        self.nc = nc
        self.ops = {e: [] for e in ENGINES}
        self.sem_handles = {}
        self.counts = {}

    AUTO_CHAIN = {"vector": "dv", "scalar": "ac", "gpsimd": "gp"}

    def init_sems(self, es):
        self._es = es
        for name in ("pe", "dv", "ac", "gp", "sy"):
            self.sem_handles[name] = es.enter_context(self.nc.semaphore(name))
            self.counts[name] = 0
        self.last_chain = {}

    def sem(self, name):
        if name not in self.sem_handles:
            self.sem_handles[name] = self._es.enter_context(self.nc.semaphore(name))
            self.counts[name] = 0
        return name

    def emit(self, engine, fn, waits=(), inc=None, inc_amt=1, chain=True):
        wl = [(s, v) for (s, v) in waits if s is not None]
        chained = chain and engine in self.AUTO_CHAIN
        if chained:
            own = self.AUTO_CHAIN[engine]
            if inc is None:
                inc, inc_amt = own, 1
            prev = self.last_chain.get(engine)
            if prev is not None:
                wl.append(prev)
        ev = None
        if inc is not None:
            self.counts[inc] += inc_amt
            ev = (inc, self.counts[inc])
        if chained and inc == self.AUTO_CHAIN[engine]:
            self.last_chain[engine] = ev
        self.ops[engine].append((tuple(wl), fn, inc, inc_amt))
        return ev

    def dma(self, engine, stream, out, in_, waits=(), **kw):
        q = self.sem(stream)
        return self.emit(engine,
                         lambda e, out=out, in_=in_, kw=kw: e.dma_start(out=out, in_=in_, **kw),
                         waits=waits, inc=q, inc_amt=16, chain=False)

    def now(self, sem):
        return (sem, self.counts.get(sem, 0))

    def replay(self, engine, eng):
        sems = self.sem_handles
        waited = {}
        for wl, fn, inc, inc_amt in self.ops[engine]:
            for s, v in sorted(wl):
                if waited.get(s, 0) >= v:
                    continue
                eng.wait_ge(sems[s], v)
                waited[s] = v
            inst = fn(eng)
            if inc is not None and inst is not None:
                inst.then_inc(sems[inc], inc_amt)

    def finalize_block(self, block):
        for engine in ENGINES:
            meth = getattr(block, engine)

            def body(eng, engine=engine):
                self.replay(engine, eng)
            meth(body)


def _maxev(*evs):
    best = {}

    def add(ev):
        if ev is None:
            return
        if isinstance(ev, list):
            for e2 in ev:
                add(e2)
            return
        s, v = ev
        if best.get(s, -1) < v:
            best[s] = v
    for ev in evs:
        add(ev)
    return [(s, v) for s, v in best.items()]


def build_program(debug=False, phases="ABCDE"):
    nc = bass.Bass("TRN2", target_bir_lowering=False)

    xH = nc.dram_tensor("xH", [NTT, 128, 16 * TT], F32, kind="ExternalInput").ap()
    wqkT = nc.dram_tensor("wqkT", [D, 128], F32, kind="ExternalInput").ap()
    wvH = nc.dram_tensor("wvH", [128, 16 * D], F32, kind="ExternalInput").ap()
    woT = nc.dram_tensor("woT", [D, D], F32, kind="ExternalInput").ap()
    w1H = nc.dram_tensor("w1H", [32, 128, 16 * 256], F32, kind="ExternalInput").ap()
    w3H = nc.dram_tensor("w3H", [32, 128, 16 * 256], F32, kind="ExternalInput").ap()
    w2H = nc.dram_tensor("w2H", [4, 32, 128, 1024], F32, kind="ExternalInput").ap()
    h128_in = nc.dram_tensor("h128", [128, 128], BF16, kind="ExternalInput").ap()
    h64p_in = nc.dram_tensor("h64p", [128, 128], BF16, kind="ExternalInput").ap()
    idt_in = nc.dram_tensor("idt", [128, 128], BF16, kind="ExternalInput").ap()
    outT = nc.dram_tensor("outT", [D, OWN], F32, kind="ExternalOutput").ap()

    scratch_kind = "ExternalOutput" if debug else "Internal"
    v_dram = nc.dram_tensor("v_scratch", [S, D], BF16, kind=scratch_kind).ap()
    h1_dram = nc.dram_tensor("h1_scratch", [HD, OWN], BF16, kind=scratch_kind).ap()
    h2_dram = nc.dram_tensor("h2_scratch", [HD, OWN], BF16, kind=scratch_kind).ap()

    dbg = {}
    if debug:
        dbg["kf"] = nc.dram_tensor("dbg_kf", [72, S], BF16, kind="ExternalOutput").ap()
        dbg["qf"] = nc.dram_tensor("dbg_qf", [72, OWN], BF16, kind="ExternalOutput").ap()
        dbg["pt"] = nc.dram_tensor("dbg_pt", [S, OWN], BF16, kind="ExternalOutput").ap()
        dbg["x2"] = nc.dram_tensor("dbg_x2", [D, OWN], F32, kind="ExternalOutput").ap()
        dbg["y2"] = nc.dram_tensor("dbg_y2", [D, OWN], BF16, kind="ExternalOutput").ap()

    P = Plan(nc)
    es = ExitStack()
    with es:
        P.init_sems(es)

        def sb(name, shape, dtype):
            return es.enter_context(nc.sbuf_tensor(name, shape, dtype))

        ones_bf = sb("ones_bf", [128, 1], BF16)
        ones1f = sb("ones1f", [1, 128], F32)
        idt = sb("idt_t", [128, 128], BF16)
        h128 = sb("h128_t", [128, 128], BF16)
        h64p = sb("h64p_t", [128, 128], BF16)
        wqk = sb("wqk", [128, 16, 128], BF16)
        kf_aug = sb("kf_aug", [72, S], BF16)
        qf_aug = sb("qf_aug", [72, OWN], BF16)
        m_row = sb("m_row", [1, TT], F32)
        var_row = sb("var_row", [1, TT], F32)
        rstd_row = sb("rstd_row", [1, TT], F32)
        mr_row = sb("mr_row", [1, TT], F32)
        linv_row = sb("linv_row", [1, TT], F32)
        linv_sb = sb("linv_sb", [128, TT], F32)
        mx_parts = sb("mx_parts", [128, 32], F32)
        m_col = sb("m_col", [128, 1], F32)
        m_bf = sb("m_bf", [128, 1], BF16)
        xsq = [sb(f"xsq{i}", [128, TT], BF16) for i in range(2)]
        cb2 = [sb(f"cb2_{i}", [128, TT], BF16) for i in range(2)]
        v_sb = [sb(f"v_sb{i}", [128, TT], BF16) for i in range(3)]
        sg_sb = [sb(f"sg{i}", [128, TT], BF16) for i in range(4)]
        h1_sb = [sb(f"h1s{i}", [128, 1024], BF16) for i in range(2)]
        h2_sb = [sb(f"h2s{i}", [128, 1024], BF16) for i in range(4)]
        y2h = sb("y2h", [128, 16, TT], BF16)
        rb2_sb = sb("rb2_sb", [128, TT], F32)
        mb2_sb = sb("mb2_sb", [128, TT], F32)
        xr_sb = [sb(f"xr{i}", [128, 1024], F32) for i in range(2)]

        ar0 = sb("arena0", [128, 65536], U8)
        ar1 = sb("arena1", [128, 32768], U8)
        ar2 = sb("arena2", [128, 32768], U8)

        def carve(ar, off, dtype, shape):
            nb = int(np.prod(shape[1:])) * mybir.dt.size(dtype)
            ap = ar[:, off:off + nb].bitcast(dtype)
            if len(shape) == 3:
                ap = ap.rearrange("p (c t) -> p c t", c=shape[1], t=shape[2])
            return ap

        wv_r = carve(ar0, 0, BF16, [128, 16, D])
        x2 = carve(ar0, 0, F32, [128, 16, OWN])
        xs_r = [carve(ar1, i * 16384, BF16, [128, 16, TT]) for i in range(2)]
        pt_sb = carve(ar1, 0, BF16, [128, 32, TT])
        y2_sb = carve(ar1, 0, BF16, [128, 16, OWN])
        y1_r = [carve(ar2, i * 16384, BF16, [128, 16, TT]) for i in range(2)]
        oa_sb = carve(ar2, 0, BF16, [128, 16, TT])
        vt_sb = [carve(ar2, 16384 + i * 2048, BF16, [128, 1024]) for i in range(4)]
        wo_sb = [carve(ar2, 24576 + i * 2048, BF16, [128, 1024]) for i in range(3)]
        w1_sb = [carve(ar2, i * 8192, BF16, [128, 16, 256]) for i in range(2)]
        w3_sb = [carve(ar2, 16384 + i * 8192, BF16, [128, 16, 256]) for i in range(2)]

        hp_sb = [carve(ar2, 0 + i * 4096, BF16, [128, 2048]) for i in range(3)]

        w2_sb = [carve(ar2, 16384 + i * 2048, BF16, [128, 1024]) for i in range(3)]
        x3_sb = [carve(ar2, 22528 + i * 4096, F32, [128, 1024]) for i in range(2)]

        pair = [es.enter_context(nc.psum_tensor(f"pp{i}", [128, 1024], F32))
                for i in range(4)]
        bank = [pair[i // 2][:, (i % 2) * TT:(i % 2 + 1) * TT] for i in range(8)]

        for i, v in enumerate((EPS, -0.5, FWHT_SCALE)):
            ct = sb(f"const{i}", [128, 1], F32)
            P.emit("gpsimd", lambda e, ct=ct, v=v: e.memset(ct[:, :], float(v)))
            nc.const_aps.register(F32, float(v), ct[:, :])
        P.emit("gpsimd", lambda e: e.memset(ones_bf[:, :], 1.0))
        P.emit("gpsimd", lambda e: e.memset(ones1f[:, :], 1.0))
        P.emit("gpsimd", lambda e: e.memset(kf_aug[64:65, :], 1.0))
        ev_memset = P.now("gp")
        ev_x0 = P.dma("gpsimd", "x0", xs_r[0][:, :, :].rearrange("p c t -> p (c t)"), xH[0, :, :])
        wqk_src = wqkT.rearrange("(c p) q -> p c q", c=16, p=128)
        ev_wqk = P.dma("gpsimd", "wqk", wqk[:, :, :], wqk_src)
        ev_x1 = P.dma("gpsimd", "x1", xs_r[1][:, :, :].rearrange("p c t -> p (c t)"), xH[1, :, :])
        P.dma("gpsimd", "init", idt[:, :], idt_in[:, :])
        P.dma("gpsimd", "init", h128[:, :], h128_in[:, :])
        P.dma("gpsimd", "init", h64p[:, :], h64p_in[:, :])
        wv_src = wvH.rearrange("p (c o) -> p c o", c=16, o=D)
        ev_wv = None
        for h in range(4):
            ev_wv = P.dma("gpsimd", "wv", wv_r[:, 4 * h:4 * h + 4, :], wv_src[:, 4 * h:4 * h + 4, :])
        ev_init3 = P.now("init")

        state = {}

        def phase_a():
            ev_bank_free = [None] * 8
            ev_xs_free = [None, None]
            ev_y1_free = [None, None]
            ev_xsq_free = [None, None]
            ev_vsb_free = [None, None, None]
            ctx = {"prev_bc_r": None, "prev_bc_mr": None, "prev_rstd": None,
                   "ev_apply": [None] * NTT, "ev_ld": [None] * NTT}
            ev_kf = [None] * NTT

            def load_x(tau):
                if tau == 0:
                    ctx["ev_ld"][0] = ev_x0
                    return
                if tau == 1:
                    ctx["ev_ld"][1] = ev_x1
                    return
                xs = xs_r[tau % 2]
                ctx["ev_ld"][tau] = P.dma(
                    "gpsimd", f"x{tau % 2}", xs[:, :, :].rearrange("p c t -> p (c t)"),
                    xH[tau, :, :], waits=_maxev(ev_xs_free[tau % 2]))

            def stats(tau):
                xs = xs_r[tau % 2]
                ev_ld = ctx["ev_ld"][tau]
                ev_sum = None
                ev_sumsq = None
                ev_sq = [None] * 16
                for c in range(16):
                    ev_sq[c] = P.emit(
                        "scalar",
                        lambda e, c=c, xs=xs: e.activation(xsq[c % 2][:, :], xs[:, c, :], AF.Square),
                        waits=_maxev(ev_ld, ev_xsq_free[c % 2]), inc="ac")
                    ev_sum = P.emit("tensor",
                                    lambda e, c=c, xs=xs: e.matmul(bank[0][0:1, :], ones_bf[:, :], xs[:, c, :],
                                                                   start=(c == 0), stop=(c == 15)),
                                    waits=_maxev(ev_ld if c == 0 else None,
                                                 ev_bank_free[0] if c == 0 else None,
                                                 ev_memset if (tau == 0 and c == 0) else None,
                                                 ev_wqk if (tau == 0 and c == 0) else None,
                                                 ev_init3 if (tau == 0 and c == 0) else None),
                                    inc="pe")
                    ev_sumsq = P.emit("tensor",
                                      lambda e, c=c: e.matmul(bank[1][0:1, :], ones_bf[:, :], xsq[c % 2][:, :],
                                                              start=(c == 0), stop=(c == 15)),
                                      waits=_maxev(ev_sq[c], ev_bank_free[1] if c == 0 else None),
                                      inc="pe")
                    ev_xsq_free[c % 2] = ev_sumsq
                P.emit("vector", lambda e: e.tensor_scalar_mul(m_row[:, :], bank[0][0:1, :], 1.0 / D),
                       waits=_maxev(ev_sum))
                ev_m = P.emit("vector", lambda e: e.tensor_mul(var_row[:, :], m_row[:, :], m_row[:, :]),
                              waits=_maxev(ctx["prev_rstd"]), inc="dv")
                ev_bank_free[0] = ev_m
                ev_var = P.emit("vector",
                                lambda e: e.scalar_tensor_tensor(var_row[:, :], bank[1][0:1, :], 1.0 / D,
                                                                 var_row[:, :], ALU.mult, ALU.subtract),
                                waits=_maxev(ev_sumsq), inc="dv")
                ev_bank_free[1] = ev_var
                P.emit("scalar", lambda e: e.activation(var_row[:, :], var_row[:, :], AF.Ln, bias=EPS),
                       waits=_maxev(ev_var, ctx["prev_bc_r"]))
                ev_rstd = P.emit("scalar", lambda e: e.activation(rstd_row[:, :], var_row[:, :], AF.Exp, scale=-0.5),
                                 inc="ac")
                ctx["prev_rstd"] = ev_rstd
                ev_mr = P.emit("vector", lambda e: e.tensor_mul(mr_row[:, :], m_row[:, :], rstd_row[:, :]),
                               waits=_maxev(ev_rstd, ctx["prev_bc_mr"]), inc="dv")
                ctx["stats_ev"] = (ev_rstd, ev_mr)

            def bcast(tau):
                ev_rstd, ev_mr = ctx["stats_ev"]
                ev_bc_r = P.emit("tensor",
                                 lambda e: e.matmul(bank[2][:, :], ones1f[:, :], rstd_row[:, :], start=True, stop=True),
                                 waits=_maxev(ev_rstd, ev_bank_free[2]), inc="pe")
                ev_bc_mr = P.emit("tensor",
                                  lambda e: e.matmul(bank[3][:, :], ones1f[:, :], mr_row[:, :], start=True, stop=True),
                                  waits=_maxev(ev_mr, ev_bank_free[3]), inc="pe")
                ctx["prev_bc_r"] = ev_bc_r
                ctx["prev_bc_mr"] = ev_bc_mr
                ctx["bc_ev"] = (ev_bc_r, ev_bc_mr)

            def apply_ln(tau):
                xs = xs_r[tau % 2]
                y1 = y1_r[tau % 2]
                ev_bc_r, ev_bc_mr = ctx["bc_ev"]
                rb = bank[2][:, :].unsqueeze(1).broadcast_to([128, 16, TT])
                mb = bank[3][:, :].unsqueeze(1).broadcast_to([128, 16, TT])
                P.emit("vector",
                       lambda e, xs=xs, y1=y1, rb=rb: e.tensor_tensor(y1[:, :, :], xs[:, :, :], rb, ALU.mult),
                       waits=_maxev(ev_bc_r, ev_y1_free[tau % 2]))
                ev_ap = P.emit("vector",
                               lambda e, y1=y1, mb=mb: e.tensor_tensor(y1[:, :, :], y1[:, :, :], mb, ALU.subtract),
                               waits=_maxev(ev_bc_mr), inc="dv")
                ctx["ev_apply"][tau] = ev_ap
                ev_bank_free[2] = ev_ap
                ev_bank_free[3] = ev_ap
                ev_xs_free[tau % 2] = ev_ap

            def qk(tau):
                y1 = y1_r[tau % 2]
                ev_qk = None
                for c in range(16):
                    ev_qk = P.emit("tensor",
                                   lambda e, c=c, y1=y1: e.matmul(bank[0][:, :], wqk[:, c, :], y1[:, c, :],
                                                                  start=(c == 0), stop=(c == 15)),
                                   waits=_maxev(ctx["ev_apply"][tau],
                                                ev_bank_free[0]) if c == 0 else (),
                                   inc=("pe" if c == 15 else None))
                ev_kfc = P.emit("vector",
                                lambda e, tau=tau: e.tensor_copy(kf_aug[0:64, tau * TT:(tau + 1) * TT], bank[0][64:128, :]),
                                waits=_maxev(ev_qk), inc="dv")
                if tau < 2:
                    ev_kfc = P.emit("vector",
                                    lambda e, tau=tau: e.tensor_copy(qf_aug[0:64, tau * TT:(tau + 1) * TT], bank[0][0:64, :]),
                                    inc="dv")
                ev_bank_free[0] = ev_kfc
                ev_kf[tau] = ev_kfc

            def v_half(tau, half):
                y1 = y1_r[tau % 2]
                ev_pe_v = None
                for g in range(8):
                    ts, ob = divmod(half * 8 + g, 4)
                    k = (4 * ts + ob) % 3
                    pv = 4 + (4 * ts + ob) % 4
                    for c in range(16):
                        ev_pe_v = P.emit("tensor",
                               lambda e, pv=pv, c=c, ts=ts, ob=ob, y1=y1: e.matmul(
                                   bank[pv][:, :],
                                   y1[:, c, ts * 128:(ts + 1) * 128],
                                   wv_r[:, c, ob * TT:(ob + 1) * TT],
                                   start=(c == 0), stop=(c == 15)),
                               waits=_maxev(ctx["ev_apply"][tau],
                                            (ev_wv if (tau == 0 and ts == 0 and ob == 0) else None),
                                            ev_bank_free[pv]) if c == 0 else (),
                               inc=("pe" if c == 15 else None))
                    ev_ev = P.emit("scalar",
                                   lambda e, k=k, pv=pv: e.copy(v_sb[k][:, :], bank[pv][:, :]),
                                   waits=_maxev(ev_pe_v, ev_vsb_free[k]), inc="ac")
                    ev_bank_free[pv] = ev_ev
                    ev_st = P.dma("scalar", f"v{k}",
                                  v_dram[tau * TT + ts * 128: tau * TT + (ts + 1) * 128,
                                         ob * TT:(ob + 1) * TT],
                                  v_sb[k][:, :], waits=_maxev(ev_ev))
                    ev_vsb_free[k] = ev_st
                if half == 1:
                    ev_y1_free[tau % 2] = ev_pe_v

            load_x(0)
            stats(0)
            bcast(0)
            apply_ln(0)
            load_x(1)
            for tau in range(NTT):
                qk(tau)
                if tau + 1 < NTT:
                    stats(tau + 1)
                if tau + 2 < NTT:
                    load_x(tau + 2)
                v_half(tau, 0)
                if tau + 1 < NTT:
                    bcast(tau + 1)
                    apply_ln(tau + 1)
                v_half(tau, 1)

            state["ev_kf"] = ev_kf
            state["a_pe"] = P.now("pe")
            state["a_dv"] = P.now("dv")
            state["a_ac"] = P.now("ac")
            state["a_vst"] = [P.now("v0"), P.now("v1"), P.now("v2")]
            if debug:
                ev = P.emit("vector", lambda e: e.nop(), inc="dv")
                P.dma("gpsimd", "dbg", dbg["kf"][0:64, :], kf_aug[0:64, :], waits=[ev])
                P.dma("gpsimd", "dbg", dbg["qf"][0:64, :], qf_aug[0:64, :], waits=[ev])

        def phase_b():
            ev_kf_all = _maxev(state["ev_kf"], state["a_ac"], state["a_dv"])
            ev_bank_free = [None] * 8
            ev_vt_free = [None, None, None, None]
            ev_wo_free = [None, None, None]
            ev_xr_free = [None, None]
            ev_pt_free = None
            for hf in range(2):
                i0 = hf * TT
                for ic in range(4):
                    ii = i0 + ic * 128
                    for j in range(8):
                        bk = (ic * 8 + j) % 4
                        ev_s = P.emit("tensor",
                               lambda e, bk=bk, ii=ii, j=j: e.matmul(
                                   bank[bk][:, :],
                                   qf_aug[0:64, ii:ii + 128],
                                   kf_aug[0:64, j * TT:(j + 1) * TT],
                                   start=True, stop=True),
                               waits=_maxev(ev_kf_all if (hf == 0 and ic == 0 and j < 4) else None,
                                            ev_bank_free[bk]), inc="pe")
                        ev_rm = P.emit("vector",
                                       lambda e, bk=bk, ic=ic, j=j: e.reduce_max(
                                           mx_parts[:, ic * 8 + j:ic * 8 + j + 1], bank[bk][:, :], axis=AX),
                                       waits=_maxev(ev_s), inc="dv")
                        ev_bank_free[bk] = ev_rm
                b4v = bank[4][:, :].bitcast(BF16)
                ev_tr4 = [None] * 4
                for ic in range(4):
                    P.emit("vector",
                           lambda e, ic=ic: e.reduce_max(m_col[:, :], mx_parts[:, ic * 8:(ic + 1) * 8], axis=AX))
                    ev_mb = P.emit("vector",
                                   lambda e, ic=ic: e.tensor_copy(m_bf[:, :], m_col[:, :]),
                                   waits=_maxev(ev_tr4[ic - 1] if ic > 0 else None),
                                   inc="dv")
                    ev_tr4[ic] = P.emit("tensor",
                                   lambda e, ic=ic, b4v=b4v: e.transpose(
                                       b4v[0:1, ic * 128:(ic + 1) * 128], m_bf[:, :], idt[:, :]),
                                   waits=_maxev(ev_mb, ev_bank_free[4] if ic == 0 else None), inc="pe")
                    ev_bank_free[4] = ev_tr4[ic]
                ev_aug = P.emit("vector",
                                lambda e, i0=i0, b4v=b4v: e.tensor_scalar_mul(
                                    qf_aug[64:65, i0:i0 + TT], b4v[0:1, 0:512], -1.0),
                                waits=_maxev(*ev_tr4), inc="dv")
                ev_bank_free[4] = ev_aug
                ev_exp = [None] * 16
                for jp in range(16):
                    pp = jp % 3
                    ev_st2 = None
                    for half in range(2):
                        jc = 2 * jp + half
                        ev_st2 = P.emit("tensor",
                               lambda e, pp=pp, jc=jc, half=half, i0=i0: e.matmul(
                                   bank[2 * pp + half][:, :],
                                   kf_aug[0:65, jc * 128:(jc + 1) * 128],
                                   qf_aug[0:65, i0:i0 + TT],
                                   start=True, stop=True),
                               waits=_maxev(ev_aug if (jp == 0 and half == 0) else None,
                                            _maxev(ev_bank_free[2 * pp], ev_bank_free[2 * pp + 1])
                                            if half == 0 else None),
                               inc=("pe" if half == 1 else None))
                    ev_exp[jp] = P.emit("scalar",
                                        lambda e, pp=pp, jp=jp: e.activation(
                                            pt_sb[:, 2 * jp:2 * jp + 2, :], pair[pp][:, :], AF.Exp),
                                        waits=_maxev(ev_st2,
                                                     ev_pt_free if jp == 0 else None,
                                                     state["a_pe"] if (hf == 0 and jp == 0) else None),
                                        inc="ac")
                    ev_bank_free[2 * pp] = ev_exp[jp]
                    ev_bank_free[2 * pp + 1] = ev_exp[jp]
                ev_l = None
                for jc in range(32):
                    ev_l = P.emit("tensor",
                           lambda e, jc=jc: e.matmul(bank[6][0:1, :], ones_bf[:, :], pt_sb[:, jc, :],
                                                     start=(jc == 0), stop=(jc == 31)),
                           waits=_maxev(ev_exp[jc // 2], ev_bank_free[6] if jc == 0 else None),
                           inc=("pe" if jc == 31 else None))
                ev_linv = P.emit("vector", lambda e: e.reciprocal(linv_row[:, :], bank[6][0:1, :]),
                                 waits=_maxev(ev_l), inc="dv")
                ev_bank_free[6] = ev_linv
                ev_lbc = P.emit("tensor",
                                lambda e: e.matmul(bank[7][:, :], ones1f[:, :], linv_row[:, :], start=True, stop=True),
                                waits=_maxev(ev_linv, ev_bank_free[7]), inc="pe")
                ev_lsb = P.emit("vector", lambda e: e.tensor_copy(linv_sb[:, :], bank[7][:, :]),
                                waits=_maxev(ev_lbc), inc="dv")
                ev_bank_free[7] = ev_lsb
                if hf == 1 and "C" in phases:
                    c_half(0, ev_bank_free, ev_x2_half0)
                if debug:
                    for jc in range(32):
                        P.dma("gpsimd", "dbg", dbg["pt"][jc * 128:(jc + 1) * 128, i0:i0 + TT], pt_sb[:, jc, :],
                              waits=_maxev(ev_exp[jc // 2]))
                ev_oa_ev = [None] * 16
                for db in range(2):
                    ev_pvl = None
                    for jc in range(32):
                        k = jc % 4
                        ev_vl = P.dma("sync" if k % 2 == 0 else "scalar", f"vt{k}", vt_sb[k][:, :],
                                      v_dram[jc * 128:(jc + 1) * 128, db * 1024:(db + 1) * 1024],
                                      waits=_maxev(state["a_vst"] if (hf == 0 and db == 0 and jc < 4) else None,
                                                   state["a_pe"] if (hf == 0 and db == 0 and jc < 4) else None,
                                                   ev_vt_free[k]))
                        for dc in range(8):
                            ev_pvl = P.emit("tensor",
                                   lambda e, dc=dc, k=k, jc=jc: e.matmul(
                                       bank[dc][:, :],
                                       vt_sb[k][:, dc * 128:(dc + 1) * 128],
                                       pt_sb[:, jc, :],
                                       start=(jc == 0), stop=(jc == 31)),
                                   waits=_maxev(ev_vl if dc == 0 else None,
                                                ev_exp[jc // 2] if dc == 0 else None,
                                                ev_lsb if (jc == 0 and dc == 0) else None,
                                                ev_bank_free[dc] if jc == 0 else None),
                                   inc=("pe" if dc == 7 else None))
                        ev_vt_free[k] = ev_pvl
                    for dc in range(8):
                        ev_oa_ev[db * 8 + dc] = P.emit(
                            "vector",
                            lambda e, db=db, dc=dc: e.tensor_tensor(
                                oa_sb[:, db * 8 + dc, :], bank[dc][:, :], linv_sb[:, :], ALU.mult),
                            waits=_maxev(ev_pvl,
                                         state["a_pe"] if (hf == 0 and db == 0 and dc == 0) else None),
                            inc="dv")
                        ev_bank_free[dc] = ev_oa_ev[db * 8 + dc]
                ev_oa = _maxev(*ev_oa_ev)
                woT_src = woT.rearrange("(c p) o -> p c o", c=16, p=128)
                ev_womm = None
                for oB in range(2):
                    for dc2 in range(16):
                        k = (oB * 16 + dc2) % 3
                        ev_wol = P.dma("gpsimd", f"wo{k}", wo_sb[k][:, :],
                                       woT_src[:, dc2, oB * 1024:(oB + 1) * 1024],
                                       waits=_maxev(ev_wo_free[k],
                                                    state["a_pe"] if (hf == 0 and oB == 0 and dc2 < 3) else None))
                        for oc in range(8):
                            ev_womm = P.emit("tensor",
                                   lambda e, oc=oc, k=k, dc2=dc2: e.matmul(
                                       bank[oc][:, :],
                                       wo_sb[k][:, oc * 128:(oc + 1) * 128],
                                       oa_sb[:, dc2, :],
                                       start=(dc2 == 0), stop=(dc2 == 15)),
                                   waits=_maxev(ev_wol if oc == 0 else None,
                                                ev_oa_ev[dc2] if oc == 0 else None,
                                                ev_bank_free[oc] if dc2 == 0 else None),
                                   inc=("pe" if oc == 7 else None))
                        ev_wo_free[k] = ev_womm
                    for op2 in range(4):
                        oo = oB * 8 + 2 * op2
                        k2 = op2 % 2
                        ev_xl = P.dma("sync", f"xr{k2}", xr_sb[k2][:, :],
                                      xH[hf, :, oo * TT:(oo + 2) * TT],
                                      waits=_maxev(ev_xr_free[k2],
                                                   state["a_pe"] if (hf == 0 and oB == 0 and op2 < 2) else None))
                        ev_x2c = P.emit("vector",
                                        lambda e, oo=oo, k2=k2, op2=op2, i0=i0: e.tensor_tensor(
                                            x2[:, oo:oo + 2, i0:i0 + TT],
                                            xr_sb[k2][:, :].rearrange("p (j t) -> p j t", j=2),
                                            pair[op2][:, :].rearrange("p (j t) -> p j t", j=2),
                                            ALU.add),
                                        waits=_maxev(ev_xl, ev_womm), inc="dv")
                        ev_xr_free[k2] = ev_x2c
                        ev_bank_free[2 * op2] = ev_x2c
                        ev_bank_free[2 * op2 + 1] = ev_x2c
                if hf == 0:
                    ev_x2_half0 = P.now("dv")
                else:
                    state["b_wo_pe"] = P.now("pe")
                    if "C" in phases:
                        c_half(1, ev_bank_free, P.now("dv"))
                ev_pt_free = _maxev(P.now("pe"),
                                    P.now("dbg") if (debug and "dbg" in P.counts) else None)
            state["b_x2"] = P.now("dv")
            state["b_pe"] = P.now("pe")
            state["b_ac"] = P.now("ac")
            if debug:
                ev = state["b_x2"]
                for oo in range(16):
                    P.dma("gpsimd", "dbg", dbg["x2"][oo * 128:(oo + 1) * 128, :], x2[:, oo, :], waits=_maxev(ev))

        cctx = {"prev_bc": None, "prev_rstd": None, "y2_ev": [None, None]}

        def c_half(tt, ev_bank_free, ev_x2_half):
            ev_cast_pe = [None, None]
            ev_sq_pe = [None, None]
            ev_sq = [None] * 16
            ev_sums = None
            for c in range(16):
                ev_cast = P.emit("vector",
                                 lambda e, c=c, tt=tt: e.tensor_copy(cb2[c % 2][:, :], x2[:, c, tt * TT:(tt + 1) * TT]),
                                 waits=_maxev(ev_x2_half if c == 0 else None,
                                              state["a_vst"] if (tt == 0 and c == 0) else None,
                                              ev_cast_pe[c % 2], ev_sq[c - 2] if c >= 2 else None),
                                 inc="dv")
                ev_sq[c] = P.emit("scalar",
                                  lambda e, c=c: e.activation(xsq[c % 2][:, :], cb2[c % 2][:, :], AF.Square),
                                  waits=_maxev(ev_cast, ev_sq_pe[c % 2]),
                                  inc="ac")
                ev_cast_pe[c % 2] = P.emit(
                    "tensor",
                    lambda e, c=c: e.matmul(bank[0][0:1, :], ones_bf[:, :], cb2[c % 2][:, :],
                                            start=(c == 0), stop=(c == 15)),
                    waits=_maxev(ev_cast, ev_bank_free[0] if c == 0 else None), inc="pe")
                ev_sq_pe[c % 2] = P.emit(
                    "tensor",
                    lambda e, c=c: e.matmul(bank[1][0:1, :], ones_bf[:, :], xsq[c % 2][:, :],
                                            start=(c == 0), stop=(c == 15)),
                    waits=_maxev(ev_sq[c], ev_bank_free[1] if c == 0 else None), inc="pe")
                ev_sums = ev_sq_pe[c % 2]
            P.emit("vector", lambda e: e.tensor_scalar_mul(m_row[:, :], bank[0][0:1, :], 1.0 / D),
                   waits=_maxev(ev_sums))
            ev_m = P.emit("vector", lambda e: e.tensor_mul(var_row[:, :], m_row[:, :], m_row[:, :]),
                          waits=_maxev(cctx["prev_rstd"]), inc="dv")
            ev_bank_free[0] = ev_m
            ev_var = P.emit("vector",
                            lambda e: e.scalar_tensor_tensor(var_row[:, :], bank[1][0:1, :], 1.0 / D,
                                                             var_row[:, :], ALU.mult, ALU.subtract),
                            inc="dv")
            ev_bank_free[1] = ev_var
            P.emit("scalar", lambda e: e.activation(var_row[:, :], var_row[:, :], AF.Ln, bias=EPS),
                   waits=_maxev(ev_var, cctx["prev_bc"]))
            ev_rstd = P.emit("scalar", lambda e: e.activation(rstd_row[:, :], var_row[:, :], AF.Exp, scale=-0.5),
                             inc="ac")
            cctx["prev_rstd"] = ev_rstd
            ev_mr = P.emit("vector", lambda e: e.tensor_mul(mr_row[:, :], m_row[:, :], rstd_row[:, :]),
                           waits=_maxev(ev_rstd, cctx["prev_bc"]), inc="dv")
            ev_bc_r = P.emit("tensor",
                             lambda e: e.matmul(bank[2][:, :], ones1f[:, :], rstd_row[:, :], start=True, stop=True),
                             waits=_maxev(ev_rstd, ev_bank_free[2]), inc="pe")
            ev_bc_mr = P.emit("tensor",
                              lambda e: e.matmul(bank[3][:, :], ones1f[:, :], mr_row[:, :], start=True, stop=True),
                              waits=_maxev(ev_mr, ev_bank_free[3]), inc="pe")
            cctx["prev_bc"] = _maxev(ev_bc_r, ev_bc_mr)
            ev_cp_r = P.emit("vector", lambda e: e.tensor_copy(rb2_sb[:, :], bank[2][:, :]),
                             waits=_maxev(ev_bc_r), inc="dv")
            ev_cp_m = P.emit("vector", lambda e: e.tensor_copy(mb2_sb[:, :], bank[3][:, :]),
                             waits=_maxev(ev_bc_mr), inc="dv")
            ev_bank_free[2] = ev_cp_r
            ev_bank_free[3] = ev_cp_m
            rb = rb2_sb[:, :].unsqueeze(1).broadcast_to([128, 16, TT])
            mb = mb2_sb[:, :].unsqueeze(1).broadcast_to([128, 16, TT])
            y2v = y2h[:, :, :] if tt == 0 else y2_sb[:, :, TT:2 * TT]
            P.emit("vector",
                   lambda e, tt=tt, rb=rb, y2v=y2v: e.tensor_tensor(
                       y2v, x2[:, :, tt * TT:(tt + 1) * TT], rb, ALU.mult),
                   waits=_maxev((P.now("dbg") if (debug and "dbg" in P.counts) else None)
                                if tt == 1 else None))
            last = P.emit("vector",
                          lambda e, y2v=y2v, mb=mb: e.tensor_tensor(y2v, y2v, mb, ALU.subtract),
                          inc="dv")
            cctx["y2_ev"][tt] = last

        def phase_c():
            state["c_y2_tt"] = [cctx["y2_ev"][0], cctx["y2_ev"][1]]
            state["c_y2"] = _maxev(cctx["y2_ev"][0], cctx["y2_ev"][1])
            state["c_pe"] = P.now("pe")
            if debug:
                ev = state["c_y2"]
                for c in range(16):
                    P.dma("gpsimd", "dbg", dbg["y2"][c * 128:(c + 1) * 128, :], y2_sb[:, c, :], waits=_maxev(ev))

        def phase_d():
            ev_bank_free = [None] * 8
            ev_w_free = [None, None]
            ev_sg_free = [None] * 4
            ev_h1_free = [None, None]
            ev_y2 = state["c_y2"]
            pend = []

            def emit_h128(a_prev, ev_hids):
                kb = a_prev % 2
                ks0 = 2 * (a_prev % 2)
                ev_ev = [None, None]
                for tt in range(2):
                    bh = 4 + 2 * kb + tt
                    ev_h128 = P.emit("tensor",
                           lambda e, bh=bh, ks=ks0 + tt: e.matmul(bank[bh][:, :], h128[:, :], sg_sb[ks][:, :],
                                                                  start=True, stop=True),
                           waits=_maxev(ev_hids[tt], ev_bank_free[bh]), inc="pe")
                    ev_sg_free[ks0 + tt] = ev_h128
                    ev_ev[tt] = P.emit("scalar",
                                       lambda e, bh=bh, kb=kb, tt=tt: e.copy(
                                           h1_sb[kb][:, tt * TT:(tt + 1) * TT], bank[bh][:, :]),
                                       waits=_maxev(ev_h128, ev_h1_free[kb] if tt == 0 else None), inc="ac")
                    ev_bank_free[bh] = ev_ev[tt]
                h1_dst = h1_dram.rearrange("(b s a) t -> s b (a t)", b=64, s=2, a=64)
                ev_h1st = P.dma("sync", f"h1_{kb}",
                                h1_dst[:, :, a_prev * OWN:(a_prev + 1) * OWN],
                                h1_sb[kb][:, :], waits=_maxev(ev_ev[0], ev_ev[1]))
                ev_h1_free[kb] = ev_h1st

            for a2 in range(32):
                k = a2 % 2
                ev_w1 = P.dma("gpsimd", f"w1_{k}", w1_sb[k][:, :, :].rearrange("p c t -> p (c t)"),
                              w1H[a2, :, :],
                              waits=_maxev(ev_w_free[k],
                                           state["b_wo_pe"] if a2 < 2 else None,
                                           (P.now("dbg") if (debug and "dbg" in P.counts) else None)
                                           if a2 < 2 else None,
                                           [P.now("h1_0"), P.now("h1_1")] if a2 < 2 else None))
                ev_w3 = P.dma("gpsimd", f"w3_{k}", w3_sb[k][:, :, :].rearrange("p c t -> p (c t)"),
                              w3H[a2, :, :],
                              waits=_maxev(state["b_wo_pe"] if a2 < 2 else None,
                                           state["b_x2"] if a2 < 2 else None))
                for j in range(2):
                    a = 2 * a2 + j
                    ks0 = 2 * (a % 2)
                    order = ([(c, tt) for tt in range(2) for c in range(16)]
                             if a == 0 else
                             [(c, tt) for c in range(16) for tt in range(2)])
                    def y2rhs(c, tt):
                        return y2h[:, c, :] if tt == 0 else y2_sb[:, c, TT:2 * TT]
                    ev_g = None
                    for c, tt in order:
                        ev_g = P.emit("tensor",
                               lambda e, c=c, k=k, j=j, tt=tt: e.matmul(
                                   bank[tt][:, :], w1_sb[k][:, c, j * 128:(j + 1) * 128],
                                   y2rhs(c, tt),
                                   start=(c == 0), stop=(c == 15)),
                               waits=_maxev(ev_w1 if (c == 0 and tt == 0) else None,
                                            state["c_y2_tt"][tt] if (a == 0 and c == 0) else None,
                                            _maxev(ev_bank_free[0], ev_bank_free[1])
                                            if (c == 0 and tt == 0) else None),
                               inc=("pe" if (c == 15 and tt == 1) else None))
                    if pend:
                        emit_h128(*pend.pop())
                    ev_lm = None
                    for c, tt in order:
                        ev_lm = P.emit("tensor",
                               lambda e, c=c, k=k, j=j, tt=tt: e.matmul(
                                   bank[2 + tt][:, :], w3_sb[k][:, c, j * 128:(j + 1) * 128],
                                   y2rhs(c, tt),
                                   start=(c == 0), stop=(c == 15)),
                               waits=_maxev(ev_w3 if (c == 0 and tt == 0) else None,
                                            _maxev(ev_bank_free[2], ev_bank_free[3])
                                            if (c == 0 and tt == 0) else None),
                               inc=("pe" if (c == 15 and tt == 1) else None))
                    ev_hids = [None, None]
                    for tt in range(2):
                        ev_silu = P.emit("scalar",
                                         lambda e, tt=tt, ks=ks0 + tt: e.activation(sg_sb[ks][:, :], bank[tt][:, :], AF.Silu),
                                         waits=_maxev(ev_g, ev_sg_free[ks0 + tt]), inc="ac")
                        if tt == 1:
                            ev_bank_free[0] = ev_silu
                            ev_bank_free[1] = ev_silu
                        ev_hids[tt] = P.emit("vector",
                                             lambda e, tt=tt, ks=ks0 + tt: e.tensor_tensor(
                                                 sg_sb[ks][:, :], sg_sb[ks][:, :], bank[2 + tt][:, :], ALU.mult),
                                             waits=_maxev(ev_silu, ev_lm), inc="dv")
                        if tt == 1:
                            ev_bank_free[2] = ev_hids[1]
                            ev_bank_free[3] = ev_hids[1]
                    pend.append((a, ev_hids))
                ev_w_free[k] = _maxev(P.now("pe"))
            emit_h128(*pend.pop())
            state["d_h1st"] = [P.now("h1_0"), P.now("h1_1")]
            state["d_ac"] = P.now("ac")
            state["d_dv"] = P.now("dv")
            state["d_pe"] = P.now("pe")

        def phase_e():
            h1_src = h1_dram.rearrange("(b p) t -> p b t", p=128)
            h2_src = h2_dram.rearrange("(b p) t -> p b t", p=128)
            ev_hp_free = [None, None, None]
            ev_h2_free = [None, None, None, None]
            ev_bank_free = [None] * 8
            ev_h2st = [None] * 32
            for bp in range(32):
                k = bp % 3
                ev_l = P.dma("sync", f"hp{k}", hp_sb[k][:, :],
                             h1_src[:, 2 * bp:2 * bp + 2, :],
                             waits=_maxev(state["d_h1st"] if bp < 3 else None,
                                          state["d_pe"] if bp < 3 else None,
                                          ev_hp_free[k]))
                for bj in range(2):
                    b = 2 * bp + bj
                    kb = b % 4
                    pp = b % 4
                    ev_mm = None
                    for half in range(2):
                        ev_mm = P.emit("tensor",
                               lambda e, pp=pp, half=half, k=k, bj=bj: e.matmul(
                                   bank[2 * pp + half][:, :], h64p[:, :],
                                   hp_sb[k][:, bj * 1024 + half * TT: bj * 1024 + (half + 1) * TT],
                                   start=True, stop=True),
                               waits=_maxev(ev_l if half == 0 else None,
                                            _maxev(ev_bank_free[2 * pp], ev_bank_free[2 * pp + 1])
                                            if half == 0 else None,
                                            _maxev(state["d_ac"], state["d_dv"]) if b < 4 else None),
                               inc=("pe" if half == 1 else None))
                    if bj == 1:
                        ev_hp_free[k] = ev_mm
                    if bj == 0:
                        ev_ev = P.emit("scalar",
                                       lambda e, pp=pp, kb=kb: e.activation(
                                           h2_sb[kb][:, :], pair[pp][:, :], AF.Copy, scale=FWHT_SCALE),
                                       waits=_maxev(ev_mm, ev_h2_free[kb]), inc="ac")
                    else:
                        ev_ev = P.emit("vector",
                                       lambda e, pp=pp, kb=kb: e.tensor_scalar_mul(
                                           h2_sb[kb][:, :], pair[pp][:, :], FWHT_SCALE),
                                       waits=_maxev(ev_mm, ev_h2_free[kb]), inc="dv")
                    ev_bank_free[2 * pp] = ev_ev
                    ev_bank_free[2 * pp + 1] = ev_ev
                    ev_st = P.dma("scalar" if bj == 0 else "sync", f"h2_{kb}",
                                  h2_dram[b * 128:(b + 1) * 128, :], h2_sb[kb][:, :],
                                  waits=_maxev(ev_ev))
                    if bj == 0:
                        ev_h2st[bp] = [ev_st]
                    else:
                        ev_h2st[bp] = ev_h2st[bp] + [ev_st]
                    ev_h2_free[kb] = ev_st
            ev_h2st_all = [P.now(f"h2_{i}") for i in range(4)]
            ev_w2_free = [None, None, None]
            ev_x3_free = [None, None]
            for oB in range(4):
                for bp in range(32):
                    k = bp % 3
                    ev_wl = P.dma("gpsimd", f"w2_{k}", w2_sb[k][:, :],
                                  w2H[oB, bp, :, :],
                                  waits=_maxev(ev_w2_free[k],
                                               state["d_pe"] if (oB == 0 and bp < 3) else None,
                                               state["b_x2"] if (oB == 0 and bp < 3) else None))
                    ev_hl = P.dma("sync", f"hp{k}", hp_sb[k][:, :],
                                  h2_src[:, 2 * bp:2 * bp + 2, :],
                                  waits=_maxev(ev_h2st[bp] if oB == 0 else None,
                                               ev_h2st_all if (oB == 0 and bp < 3) else None,
                                               ev_hp_free[k]))
                    evp = None
                    for bj in range(2):
                        for oc in range(4):
                            for tt in range(2):
                                bk = oc * 2 + tt
                                evp = P.emit("tensor",
                                       lambda e, bk=bk, oc=oc, tt=tt, k=k, bj=bj, bp=bp: e.matmul(
                                           bank[bk][:, :],
                                           w2_sb[k][:, bj * 512 + oc * 128: bj * 512 + (oc + 1) * 128],
                                           hp_sb[k][:, bj * 1024 + tt * TT: bj * 1024 + (tt + 1) * TT],
                                           start=(bp == 0 and bj == 0), stop=(bp == 31 and bj == 1)),
                                       waits=_maxev(ev_wl if (bj == 0 and oc == 0 and tt == 0) else None,
                                                    ev_hl if (bj == 0 and oc == 0 and tt == 0) else None,
                                                    ev_bank_free[bk]
                                                    if (bp == 0 and bj == 0) else None),
                                       inc=("pe" if (bj == 1 and tt == 1 and (bp == 31 or oc == 3)) else None))
                    if bp == 31:
                        for oc2 in range(4):
                            state[f"e2_stop{oc2}"] = ("pe", P.counts["pe"] - (3 - oc2))
                    ev_w2_free[k] = evp
                    ev_hp_free[k] = evp
                for oc in range(4):
                    oo = oB * 4 + oc
                    k3 = oc % 2
                    ev_add = P.emit("vector",
                                    lambda e, oc=oc, oo=oo, k3=k3: e.tensor_tensor(
                                        x3_sb[k3][:, :], x2[:, oo, :], pair[oc][:, :], ALU.add),
                                    waits=_maxev(state[f"e2_stop{oc}"], ev_x3_free[k3],
                                                 state["b_x2"] if (oB == 0 and oc == 0) else None),
                                    inc="dv")
                    ev_bank_free[2 * oc] = ev_add
                    ev_bank_free[2 * oc + 1] = ev_add
                    ev_out = P.dma("scalar", f"x3_{k3}",
                                   outT[oo * 128:(oo + 1) * 128, :],
                                   x3_sb[k3][:, :], waits=_maxev(ev_add))
                    ev_x3_free[k3] = ev_out
            state["out_done"] = [P.now("x3_0"), P.now("x3_1")]

        if "A" in phases:
            phase_a()
        if "B" in phases:
            phase_b()
        if "C" in phases:
            phase_c()
        if "D" in phases:
            phase_d()
        if "E" in phases:
            phase_e()

        if "E" in phases:
            P.emit("sync", lambda e: e.nop(),
                   waits=_maxev(state["out_done"],
                                (P.now("dbg") if "dbg" in P.counts else None) if debug else None))
        else:
            ev = P.emit("gpsimd", lambda e: e.memset(x3_sb[0][:, :], 0.0),
                        waits=_maxev(state.get("b_x2"), state.get("c_y2"),
                                     state.get("a_vst"), P.now("dv"), P.now("ac"),
                                     P.now("pe")),
                        inc="gp")
            for oo in range(16):
                P.dma("sync", "ph", outT[oo * 128:(oo + 1) * 128, :],
                      x3_sb[0][:, :], waits=_maxev(ev))
            P.emit("sync", lambda e: e.nop(),
                   waits=_maxev(P.now("ph"),
                                (P.now("dbg") if "dbg" in P.counts else None) if debug else None))

        with nc.Block() as block:
            P.finalize_block(block)

    return nc



def _hadamard(n):
    h = np.array([[1.0]], dtype=np.float32)
    while h.shape[0] < n:
        h = np.block([[h, h], [h, -h]])
    return h


def prep_inputs(L0_attn_hidden_states, L0_attn_Wq, L0_attn_Wk, L0_attn_Wv,
                L0_attn_Wo, L0_ffn_W1, L0_ffn_W2, L0_ffn_W3):
    x = np.asarray(L0_attn_hidden_states)
    w1T = np.asarray(L0_ffn_W1).T.astype(np.float32)
    w3T = np.asarray(L0_ffn_W3).T.astype(np.float32)
    w2T = np.asarray(L0_ffn_W2).T.astype(np.float32)
    wvT = np.asarray(L0_attn_Wv).T.astype(np.float32)
    base = {
        "wqkT": np.ascontiguousarray(
            np.concatenate([np.asarray(L0_attn_Wq), np.asarray(L0_attn_Wk)], axis=0).T).astype(np.float32),
        "woT": np.ascontiguousarray(np.asarray(L0_attn_Wo).T).astype(np.float32),
        "wvH": np.ascontiguousarray(
            wvT.reshape(16, 128, D).transpose(1, 0, 2).reshape(128, 16 * D)),
        "w1H": np.ascontiguousarray(
            w1T.reshape(16, 128, 32, 256).transpose(2, 1, 0, 3).reshape(32, 128, 16 * 256)),
        "w3H": np.ascontiguousarray(
            w3T.reshape(16, 128, 32, 256).transpose(2, 1, 0, 3).reshape(32, 128, 16 * 256)),
        "w2H": np.ascontiguousarray(
            w2T.reshape(64, 2, 32, 2, 4, 512).transpose(4, 2, 1, 0, 3, 5).reshape(4, 32, 128, 1024)),
        "idt": np.eye(128, dtype=ml_dtypes.bfloat16),
        "h128": _hadamard(128).astype(ml_dtypes.bfloat16),
    }
    h64 = _hadamard(64)
    h64p = np.zeros((128, 128), dtype=np.float32)
    h64p[0:64, 0:64] = h64
    h64p[64:128, 64:128] = h64
    base["h64p"] = h64p.astype(ml_dtypes.bfloat16)
    in_maps = []
    for core in range(8):
        b, r = core // 4, core % 4
        m = dict(base)
        xb = np.roll(x[b].T, -r * OWN, axis=1).astype(np.float32)
        m["xH"] = np.ascontiguousarray(
            xb.reshape(16, 128, 8, 512).transpose(2, 1, 0, 3).reshape(8, 128, 16 * 512))
        in_maps.append(m)
    return in_maps


def kernel(**inputs):
    in_maps = prep_inputs(**inputs)
    nc = build_program(debug=False)
    res = run_bass_kernel_spmd(nc, in_maps, core_ids=list(range(8)))
    out = np.empty((2, S, D), dtype=np.float32)
    for core in range(8):
        b, r = core // 4, core % 4
        out[b, r * OWN:(r + 1) * OWN, :] = res.results[core]["outT"].T
    return out


# version 18
# speedup vs baseline: 1.0402x; 1.0402x over previous
import numpy as np
import ml_dtypes
import concourse.bass as bass
import concourse.mybir as mybir
from concourse.bass_utils import run_bass_kernel_spmd
from contextlib import ExitStack

F32 = mybir.dt.float32
BF16 = mybir.dt.bfloat16
U8 = mybir.dt.uint8
AF = mybir.ActivationFunctionType
ALU = mybir.AluOpType
AX = mybir.AxisListType.X

D = 2048
S = 4096
HD = 8192
TT = 512
NTT = S // TT
OWN = 1024
EPS = 1e-5
FWHT_SCALE = 1.0 / float(np.sqrt(HD))

ENGINES = ("tensor", "vector", "scalar", "gpsimd", "sync")


class Plan:

    def __init__(self, nc):
        self.nc = nc
        self.ops = {e: [] for e in ENGINES}
        self.sem_handles = {}
        self.counts = {}

    AUTO_CHAIN = {"vector": "dv", "scalar": "ac", "gpsimd": "gp"}

    def init_sems(self, es):
        self._es = es
        for name in ("pe", "dv", "ac", "gp", "sy"):
            self.sem_handles[name] = es.enter_context(self.nc.semaphore(name))
            self.counts[name] = 0
        self.last_chain = {}

    def sem(self, name):
        if name not in self.sem_handles:
            self.sem_handles[name] = self._es.enter_context(self.nc.semaphore(name))
            self.counts[name] = 0
        return name

    def emit(self, engine, fn, waits=(), inc=None, inc_amt=1, chain=True):
        wl = [(s, v) for (s, v) in waits if s is not None]
        chained = chain and engine in self.AUTO_CHAIN
        if chained:
            own = self.AUTO_CHAIN[engine]
            if inc is None:
                inc, inc_amt = own, 1
            prev = self.last_chain.get(engine)
            if prev is not None:
                wl.append(prev)
        ev = None
        if inc is not None:
            self.counts[inc] += inc_amt
            ev = (inc, self.counts[inc])
        if chained and inc == self.AUTO_CHAIN[engine]:
            self.last_chain[engine] = ev
        self.ops[engine].append((tuple(wl), fn, inc, inc_amt))
        return ev

    def dma(self, engine, stream, out, in_, waits=(), **kw):
        q = self.sem(stream)
        return self.emit(engine,
                         lambda e, out=out, in_=in_, kw=kw: e.dma_start(out=out, in_=in_, **kw),
                         waits=waits, inc=q, inc_amt=16, chain=False)

    def now(self, sem):
        return (sem, self.counts.get(sem, 0))

    def replay(self, engine, eng):
        sems = self.sem_handles
        waited = {}
        for wl, fn, inc, inc_amt in self.ops[engine]:
            for s, v in sorted(wl):
                if waited.get(s, 0) >= v:
                    continue
                eng.wait_ge(sems[s], v)
                waited[s] = v
            inst = fn(eng)
            if inc is not None and inst is not None:
                inst.then_inc(sems[inc], inc_amt)

    def finalize_block(self, block):
        for engine in ENGINES:
            meth = getattr(block, engine)

            def body(eng, engine=engine):
                self.replay(engine, eng)
            meth(body)


def _maxev(*evs):
    best = {}

    def add(ev):
        if ev is None:
            return
        if isinstance(ev, list):
            for e2 in ev:
                add(e2)
            return
        s, v = ev
        if best.get(s, -1) < v:
            best[s] = v
    for ev in evs:
        add(ev)
    return [(s, v) for s, v in best.items()]


def build_program(debug=False, phases="ABCDE"):
    nc = bass.Bass("TRN2", target_bir_lowering=False)

    xH = nc.dram_tensor("xH", [NTT, 128, 16 * TT], F32, kind="ExternalInput").ap()
    wqkT = nc.dram_tensor("wqkT", [D, 128], F32, kind="ExternalInput").ap()
    wvH = nc.dram_tensor("wvH", [128, 16 * D], F32, kind="ExternalInput").ap()
    woT = nc.dram_tensor("woT", [D, D], F32, kind="ExternalInput").ap()
    w1H = nc.dram_tensor("w1H", [32, 128, 16 * 256], F32, kind="ExternalInput").ap()
    w3H = nc.dram_tensor("w3H", [32, 128, 16 * 256], F32, kind="ExternalInput").ap()
    w2H = nc.dram_tensor("w2H", [4, 32, 128, 1024], F32, kind="ExternalInput").ap()
    h128_in = nc.dram_tensor("h128", [128, 128], BF16, kind="ExternalInput").ap()
    h64p_in = nc.dram_tensor("h64p", [128, 128], BF16, kind="ExternalInput").ap()
    idt_in = nc.dram_tensor("idt", [128, 128], BF16, kind="ExternalInput").ap()
    outT = nc.dram_tensor("outT", [D, OWN], F32, kind="ExternalOutput").ap()

    scratch_kind = "ExternalOutput" if debug else "Internal"
    v_dram = nc.dram_tensor("v_scratch", [S, D], BF16, kind=scratch_kind).ap()
    h1_dram = nc.dram_tensor("h1_scratch", [HD, OWN], BF16, kind=scratch_kind).ap()
    h2_dram = nc.dram_tensor("h2_scratch", [HD, OWN], BF16, kind=scratch_kind).ap()

    dbg = {}
    if debug:
        dbg["kf"] = nc.dram_tensor("dbg_kf", [72, S], BF16, kind="ExternalOutput").ap()
        dbg["qf"] = nc.dram_tensor("dbg_qf", [72, OWN], BF16, kind="ExternalOutput").ap()
        dbg["pt"] = nc.dram_tensor("dbg_pt", [S, OWN], BF16, kind="ExternalOutput").ap()
        dbg["x2"] = nc.dram_tensor("dbg_x2", [D, OWN], F32, kind="ExternalOutput").ap()
        dbg["y2"] = nc.dram_tensor("dbg_y2", [D, OWN], BF16, kind="ExternalOutput").ap()

    P = Plan(nc)
    es = ExitStack()
    with es:
        P.init_sems(es)

        def sb(name, shape, dtype):
            return es.enter_context(nc.sbuf_tensor(name, shape, dtype))

        ones_bf = sb("ones_bf", [128, 1], BF16)
        ones1f = sb("ones1f", [1, 128], F32)
        idt = sb("idt_t", [128, 128], BF16)
        h128 = sb("h128_t", [128, 128], BF16)
        h64p = sb("h64p_t", [128, 128], BF16)
        wqk = sb("wqk", [128, 16, 128], BF16)
        kf_aug = sb("kf_aug", [72, S], BF16)
        qf_aug = sb("qf_aug", [72, OWN], BF16)
        m_row = sb("m_row", [1, TT], F32)
        var_row = sb("var_row", [1, TT], F32)
        rstd_row = sb("rstd_row", [1, TT], F32)
        mr_row = sb("mr_row", [1, TT], F32)
        linv_row = sb("linv_row", [1, TT], F32)
        linv_sb = sb("linv_sb", [128, TT], F32)
        mx_parts = sb("mx_parts", [128, 32], F32)
        m_col = sb("m_col", [128, 1], F32)
        m_bf = sb("m_bf", [128, 1], BF16)
        xsq = [sb(f"xsq{i}", [128, TT], BF16) for i in range(2)]
        cb2 = [sb(f"cb2_{i}", [128, TT], BF16) for i in range(2)]
        v_sb = [sb(f"v_sb{i}", [128, TT], BF16) for i in range(3)]
        sg_sb = [sb(f"sg{i}", [128, TT], BF16) for i in range(4)]
        h1_sb = [sb(f"h1s{i}", [128, 1024], BF16) for i in range(2)]
        h2_sb = [sb(f"h2s{i}", [128, 1024], BF16) for i in range(4)]
        y2h = sb("y2h", [128, 16, TT], BF16)
        rb2_sb = sb("rb2_sb", [128, TT], F32)
        mb2_sb = sb("mb2_sb", [128, TT], F32)
        xr_sb = [sb(f"xr{i}", [128, 1024], F32) for i in range(2)]

        ar0 = sb("arena0", [128, 65536], U8)
        ar1 = sb("arena1", [128, 32768], U8)
        ar2 = sb("arena2", [128, 32768], U8)

        def carve(ar, off, dtype, shape):
            nb = int(np.prod(shape[1:])) * mybir.dt.size(dtype)
            ap = ar[:, off:off + nb].bitcast(dtype)
            if len(shape) == 3:
                ap = ap.rearrange("p (c t) -> p c t", c=shape[1], t=shape[2])
            return ap

        wv_r = carve(ar0, 0, BF16, [128, 16, D])
        x2 = carve(ar0, 0, F32, [128, 16, OWN])
        xs_r = [carve(ar1, i * 16384, BF16, [128, 16, TT]) for i in range(2)]
        pt_sb = carve(ar1, 0, BF16, [128, 32, TT])
        y2_sb = carve(ar1, 0, BF16, [128, 16, OWN])
        y1_r = [carve(ar2, i * 16384, BF16, [128, 16, TT]) for i in range(2)]
        oa_sb = carve(ar2, 0, BF16, [128, 16, TT])
        vt_sb = [carve(ar2, 16384 + i * 2048, BF16, [128, 1024]) for i in range(4)]
        wo_sb = [carve(ar2, 24576 + i * 2048, BF16, [128, 1024]) for i in range(3)]
        w1_sb = [carve(ar2, i * 8192, BF16, [128, 16, 256]) for i in range(2)]
        w3_sb = [carve(ar2, 16384 + i * 8192, BF16, [128, 16, 256]) for i in range(2)]

        hp_sb = [carve(ar2, 0 + i * 4096, BF16, [128, 2048]) for i in range(3)]

        w2_sb = [carve(ar2, 16384 + i * 2048, BF16, [128, 1024]) for i in range(3)]
        x3_sb = [carve(ar2, 22528 + i * 4096, F32, [128, 1024]) for i in range(2)]

        pair = [es.enter_context(nc.psum_tensor(f"pp{i}", [128, 1024], F32))
                for i in range(4)]
        bank = [pair[i // 2][:, (i % 2) * TT:(i % 2 + 1) * TT] for i in range(8)]

        for i, v in enumerate((EPS, -0.5, FWHT_SCALE)):
            ct = sb(f"const{i}", [128, 1], F32)
            P.emit("gpsimd", lambda e, ct=ct, v=v: e.memset(ct[:, :], float(v)))
            nc.const_aps.register(F32, float(v), ct[:, :])
        P.emit("gpsimd", lambda e: e.memset(ones_bf[:, :], 1.0))
        P.emit("gpsimd", lambda e: e.memset(ones1f[:, :], 1.0))
        P.emit("gpsimd", lambda e: e.memset(kf_aug[64:65, :], 1.0))
        ev_memset = P.now("gp")
        ev_x0 = P.dma("gpsimd", "x0", xs_r[0][:, :, :].rearrange("p c t -> p (c t)"), xH[0, :, :])
        wqk_src = wqkT.rearrange("(c p) q -> p c q", c=16, p=128)
        ev_wqk = P.dma("gpsimd", "wqk", wqk[:, :, :], wqk_src)
        ev_x1 = P.dma("gpsimd", "x1", xs_r[1][:, :, :].rearrange("p c t -> p (c t)"), xH[1, :, :])
        P.dma("gpsimd", "init", idt[:, :], idt_in[:, :])
        P.dma("gpsimd", "init", h128[:, :], h128_in[:, :])
        P.dma("gpsimd", "init", h64p[:, :], h64p_in[:, :])
        wv_src = wvH.rearrange("p (c o) -> p c o", c=16, o=D)
        ev_wv = None
        for h in range(4):
            ev_wv = P.dma("gpsimd", "wv", wv_r[:, 4 * h:4 * h + 4, :], wv_src[:, 4 * h:4 * h + 4, :])
        ev_init3 = P.now("init")

        state = {}

        def phase_a():
            ev_bank_free = [None] * 8
            ev_xs_free = [None, None]
            ev_y1_free = [None, None]
            ev_xsq_free = [None, None]
            ev_vsb_free = [None, None, None]
            ctx = {"prev_bc_r": None, "prev_bc_mr": None, "prev_rstd": None,
                   "ev_apply": [None] * NTT, "ev_ld": [None] * NTT}
            ev_kf = [None] * NTT

            def load_x(tau):
                if tau == 0:
                    ctx["ev_ld"][0] = ev_x0
                    return
                if tau == 1:
                    ctx["ev_ld"][1] = ev_x1
                    return
                xs = xs_r[tau % 2]
                ctx["ev_ld"][tau] = P.dma(
                    "gpsimd", f"x{tau % 2}", xs[:, :, :].rearrange("p c t -> p (c t)"),
                    xH[tau, :, :], waits=_maxev(ev_xs_free[tau % 2]))

            def stats(tau):
                xs = xs_r[tau % 2]
                ev_ld = ctx["ev_ld"][tau]
                ev_sum = None
                ev_sumsq = None
                ev_sq = [None] * 16
                for c in range(16):
                    ev_sq[c] = P.emit(
                        "scalar",
                        lambda e, c=c, xs=xs: e.activation(xsq[c % 2][:, :], xs[:, c, :], AF.Square),
                        waits=_maxev(ev_ld, ev_xsq_free[c % 2]), inc="ac")
                    ev_sum = P.emit("tensor",
                                    lambda e, c=c, xs=xs: e.matmul(bank[0][0:1, :], ones_bf[:, :], xs[:, c, :],
                                                                   start=(c == 0), stop=(c == 15)),
                                    waits=_maxev(ev_ld if c == 0 else None,
                                                 ev_bank_free[0] if c == 0 else None,
                                                 ev_memset if (tau == 0 and c == 0) else None,
                                                 ev_wqk if (tau == 0 and c == 0) else None,
                                                 ev_init3 if (tau == 0 and c == 0) else None),
                                    inc="pe")
                    ev_sumsq = P.emit("tensor",
                                      lambda e, c=c: e.matmul(bank[1][0:1, :], ones_bf[:, :], xsq[c % 2][:, :],
                                                              start=(c == 0), stop=(c == 15)),
                                      waits=_maxev(ev_sq[c], ev_bank_free[1] if c == 0 else None),
                                      inc="pe")
                    ev_xsq_free[c % 2] = ev_sumsq
                P.emit("vector", lambda e: e.tensor_scalar_mul(m_row[:, :], bank[0][0:1, :], 1.0 / D),
                       waits=_maxev(ev_sum))
                ev_m = P.emit("vector", lambda e: e.tensor_mul(var_row[:, :], m_row[:, :], m_row[:, :]),
                              waits=_maxev(ctx["prev_rstd"]), inc="dv")
                ev_bank_free[0] = ev_m
                ev_var = P.emit("vector",
                                lambda e: e.scalar_tensor_tensor(var_row[:, :], bank[1][0:1, :], 1.0 / D,
                                                                 var_row[:, :], ALU.mult, ALU.subtract),
                                waits=_maxev(ev_sumsq), inc="dv")
                ev_bank_free[1] = ev_var
                P.emit("scalar", lambda e: e.activation(var_row[:, :], var_row[:, :], AF.Ln, bias=EPS),
                       waits=_maxev(ev_var, ctx["prev_bc_r"]))
                ev_rstd = P.emit("scalar", lambda e: e.activation(rstd_row[:, :], var_row[:, :], AF.Exp, scale=-0.5),
                                 inc="ac")
                ctx["prev_rstd"] = ev_rstd
                ev_mr = P.emit("vector", lambda e: e.tensor_mul(mr_row[:, :], m_row[:, :], rstd_row[:, :]),
                               waits=_maxev(ev_rstd, ctx["prev_bc_mr"]), inc="dv")
                ctx["stats_ev"] = (ev_rstd, ev_mr)

            def bcast(tau):
                ev_rstd, ev_mr = ctx["stats_ev"]
                ev_bc_r = P.emit("tensor",
                                 lambda e: e.matmul(bank[2][:, :], ones1f[:, :], rstd_row[:, :], start=True, stop=True),
                                 waits=_maxev(ev_rstd, ev_bank_free[2]), inc="pe")
                ev_bc_mr = P.emit("tensor",
                                  lambda e: e.matmul(bank[3][:, :], ones1f[:, :], mr_row[:, :], start=True, stop=True),
                                  waits=_maxev(ev_mr, ev_bank_free[3]), inc="pe")
                ctx["prev_bc_r"] = ev_bc_r
                ctx["prev_bc_mr"] = ev_bc_mr
                ctx["bc_ev"] = (ev_bc_r, ev_bc_mr)

            def apply_ln(tau):
                xs = xs_r[tau % 2]
                y1 = y1_r[tau % 2]
                ev_bc_r, ev_bc_mr = ctx["bc_ev"]
                rb = bank[2][:, :].unsqueeze(1).broadcast_to([128, 16, TT])
                mb = bank[3][:, :].unsqueeze(1).broadcast_to([128, 16, TT])
                P.emit("vector",
                       lambda e, xs=xs, y1=y1, rb=rb: e.tensor_tensor(y1[:, :, :], xs[:, :, :], rb, ALU.mult),
                       waits=_maxev(ev_bc_r, ev_y1_free[tau % 2]))
                ev_ap = P.emit("vector",
                               lambda e, y1=y1, mb=mb: e.tensor_tensor(y1[:, :, :], y1[:, :, :], mb, ALU.subtract),
                               waits=_maxev(ev_bc_mr), inc="dv")
                ctx["ev_apply"][tau] = ev_ap
                ev_bank_free[2] = ev_ap
                ev_bank_free[3] = ev_ap
                ev_xs_free[tau % 2] = ev_ap

            def qk(tau):
                y1 = y1_r[tau % 2]
                ev_qk = None
                for c in range(16):
                    ev_qk = P.emit("tensor",
                                   lambda e, c=c, y1=y1: e.matmul(bank[0][:, :], wqk[:, c, :], y1[:, c, :],
                                                                  start=(c == 0), stop=(c == 15)),
                                   waits=_maxev(ctx["ev_apply"][tau],
                                                ev_bank_free[0]) if c == 0 else (),
                                   inc=("pe" if c == 15 else None))
                ev_kfc = P.emit("vector",
                                lambda e, tau=tau: e.tensor_copy(kf_aug[0:64, tau * TT:(tau + 1) * TT], bank[0][64:128, :]),
                                waits=_maxev(ev_qk), inc="dv")
                if tau < 2:
                    ev_kfc = P.emit("vector",
                                    lambda e, tau=tau: e.tensor_copy(qf_aug[0:64, tau * TT:(tau + 1) * TT], bank[0][0:64, :]),
                                    inc="dv")
                ev_bank_free[0] = ev_kfc
                ev_kf[tau] = ev_kfc

            def v_half(tau, half):
                y1 = y1_r[tau % 2]
                ev_pe_v = None
                for g in range(8):
                    ts, ob = divmod(half * 8 + g, 4)
                    k = (4 * ts + ob) % 3
                    pv = 4 + (4 * ts + ob) % 4
                    for c in range(16):
                        ev_pe_v = P.emit("tensor",
                               lambda e, pv=pv, c=c, ts=ts, ob=ob, y1=y1: e.matmul(
                                   bank[pv][:, :],
                                   y1[:, c, ts * 128:(ts + 1) * 128],
                                   wv_r[:, c, ob * TT:(ob + 1) * TT],
                                   start=(c == 0), stop=(c == 15)),
                               waits=_maxev(ctx["ev_apply"][tau],
                                            (ev_wv if (tau == 0 and ts == 0 and ob == 0) else None),
                                            ev_bank_free[pv]) if c == 0 else (),
                               inc=("pe" if c == 15 else None))
                    ev_ev = P.emit("scalar",
                                   lambda e, k=k, pv=pv: e.copy(v_sb[k][:, :], bank[pv][:, :]),
                                   waits=_maxev(ev_pe_v, ev_vsb_free[k]), inc="ac")
                    ev_bank_free[pv] = ev_ev
                    ev_st = P.dma("scalar", f"v{k}",
                                  v_dram[tau * TT + ts * 128: tau * TT + (ts + 1) * 128,
                                         ob * TT:(ob + 1) * TT],
                                  v_sb[k][:, :], waits=_maxev(ev_ev))
                    ev_vsb_free[k] = ev_st
                if half == 1:
                    ev_y1_free[tau % 2] = ev_pe_v

            load_x(0)
            stats(0)
            bcast(0)
            apply_ln(0)
            load_x(1)
            for tau in range(NTT):
                qk(tau)
                if tau + 1 < NTT:
                    stats(tau + 1)
                if tau + 2 < NTT:
                    load_x(tau + 2)
                v_half(tau, 0)
                if tau + 1 < NTT:
                    bcast(tau + 1)
                    apply_ln(tau + 1)
                v_half(tau, 1)

            state["ev_kf"] = ev_kf
            state["a_pe"] = P.now("pe")
            state["a_dv"] = P.now("dv")
            state["a_ac"] = P.now("ac")
            state["a_vst"] = [P.now("v0"), P.now("v1"), P.now("v2")]
            if debug:
                ev = P.emit("vector", lambda e: e.nop(), inc="dv")
                P.dma("gpsimd", "dbg", dbg["kf"][0:64, :], kf_aug[0:64, :], waits=[ev])
                P.dma("gpsimd", "dbg", dbg["qf"][0:64, :], qf_aug[0:64, :], waits=[ev])

        def phase_b():
            ev_kf_all = _maxev(state["ev_kf"], state["a_ac"], state["a_dv"])
            ev_bank_free = [None] * 8
            ev_vt_free = [None, None, None, None]
            ev_wo_free = [None, None, None]
            ev_xr_free = [None, None]
            ev_pt_free = None
            for hf in range(2):
                i0 = hf * TT
                for ic in range(4):
                    ii = i0 + ic * 128
                    for j in range(8):
                        bk = (ic * 8 + j) % 4
                        ev_s = P.emit("tensor",
                               lambda e, bk=bk, ii=ii, j=j: e.matmul(
                                   bank[bk][:, :],
                                   qf_aug[0:64, ii:ii + 128],
                                   kf_aug[0:64, j * TT:(j + 1) * TT],
                                   start=True, stop=True),
                               waits=_maxev(ev_kf_all if (hf == 0 and ic == 0 and j < 4) else None,
                                            ev_bank_free[bk]), inc="pe")
                        ev_rm = P.emit("vector",
                                       lambda e, bk=bk, ic=ic, j=j: e.reduce_max(
                                           mx_parts[:, ic * 8 + j:ic * 8 + j + 1], bank[bk][:, :], axis=AX),
                                       waits=_maxev(ev_s), inc="dv")
                        ev_bank_free[bk] = ev_rm
                b4v = bank[4][:, :].bitcast(BF16)
                ev_tr4 = [None] * 4
                for ic in range(4):
                    P.emit("vector",
                           lambda e, ic=ic: e.reduce_max(m_col[:, :], mx_parts[:, ic * 8:(ic + 1) * 8], axis=AX))
                    ev_mb = P.emit("vector",
                                   lambda e, ic=ic: e.tensor_copy(m_bf[:, :], m_col[:, :]),
                                   waits=_maxev(ev_tr4[ic - 1] if ic > 0 else None),
                                   inc="dv")
                    ev_tr4[ic] = P.emit("tensor",
                                   lambda e, ic=ic, b4v=b4v: e.transpose(
                                       b4v[0:1, ic * 128:(ic + 1) * 128], m_bf[:, :], idt[:, :]),
                                   waits=_maxev(ev_mb, ev_bank_free[4] if ic == 0 else None), inc="pe")
                    ev_bank_free[4] = ev_tr4[ic]
                ev_aug = P.emit("vector",
                                lambda e, i0=i0, b4v=b4v: e.tensor_scalar_mul(
                                    qf_aug[64:65, i0:i0 + TT], b4v[0:1, 0:512], -1.0),
                                waits=_maxev(*ev_tr4), inc="dv")
                ev_bank_free[4] = ev_aug
                ev_exp = [None] * 16
                for jp in range(16):
                    pp = jp % 3
                    ev_st2 = None
                    for half in range(2):
                        jc = 2 * jp + half
                        ev_st2 = P.emit("tensor",
                               lambda e, pp=pp, jc=jc, half=half, i0=i0: e.matmul(
                                   bank[2 * pp + half][:, :],
                                   kf_aug[0:65, jc * 128:(jc + 1) * 128],
                                   qf_aug[0:65, i0:i0 + TT],
                                   start=True, stop=True),
                               waits=_maxev(ev_aug if (jp == 0 and half == 0) else None,
                                            _maxev(ev_bank_free[2 * pp], ev_bank_free[2 * pp + 1])
                                            if half == 0 else None),
                               inc=("pe" if half == 1 else None))
                    ev_exp[jp] = P.emit("scalar",
                                        lambda e, pp=pp, jp=jp: e.activation(
                                            pt_sb[:, 2 * jp:2 * jp + 2, :], pair[pp][:, :], AF.Exp),
                                        waits=_maxev(ev_st2,
                                                     ev_pt_free if jp == 0 else None,
                                                     state["a_pe"] if (hf == 0 and jp == 0) else None),
                                        inc="ac")
                    ev_bank_free[2 * pp] = ev_exp[jp]
                    ev_bank_free[2 * pp + 1] = ev_exp[jp]
                ev_l = None
                for jc in range(32):
                    ev_l = P.emit("tensor",
                           lambda e, jc=jc: e.matmul(bank[6][0:1, :], ones_bf[:, :], pt_sb[:, jc, :],
                                                     start=(jc == 0), stop=(jc == 31)),
                           waits=_maxev(ev_exp[jc // 2], ev_bank_free[6] if jc == 0 else None),
                           inc=("pe" if jc == 31 else None))
                ev_linv = P.emit("vector", lambda e: e.reciprocal(linv_row[:, :], bank[6][0:1, :]),
                                 waits=_maxev(ev_l), inc="dv")
                ev_bank_free[6] = ev_linv
                ev_lbc = P.emit("tensor",
                                lambda e: e.matmul(bank[7][:, :], ones1f[:, :], linv_row[:, :], start=True, stop=True),
                                waits=_maxev(ev_linv, ev_bank_free[7]), inc="pe")
                ev_lsb = P.emit("vector", lambda e: e.tensor_copy(linv_sb[:, :], bank[7][:, :]),
                                waits=_maxev(ev_lbc), inc="dv")
                ev_bank_free[7] = ev_lsb
                if hf == 1 and "C" in phases:
                    c_half(0, ev_bank_free, ev_x2_half0)
                if debug:
                    for jc in range(32):
                        P.dma("gpsimd", "dbg", dbg["pt"][jc * 128:(jc + 1) * 128, i0:i0 + TT], pt_sb[:, jc, :],
                              waits=_maxev(ev_exp[jc // 2]))
                ev_oa_ev = [None] * 16
                for db in range(2):
                    ev_pvl = None
                    for jc in range(32):
                        k = jc % 4
                        ev_vl = P.dma("sync" if k % 2 == 0 else "scalar", f"vt{k}", vt_sb[k][:, :],
                                      v_dram[jc * 128:(jc + 1) * 128, db * 1024:(db + 1) * 1024],
                                      waits=_maxev(state["a_vst"] if (hf == 0 and db == 0 and jc < 4) else None,
                                                   state["a_pe"] if (hf == 0 and db == 0 and jc < 4) else None,
                                                   ev_vt_free[k]))
                        for dc in range(8):
                            ev_pvl = P.emit("tensor",
                                   lambda e, dc=dc, k=k, jc=jc: e.matmul(
                                       bank[dc][:, :],
                                       vt_sb[k][:, dc * 128:(dc + 1) * 128],
                                       pt_sb[:, jc, :],
                                       start=(jc == 0), stop=(jc == 31)),
                                   waits=_maxev(ev_vl if dc == 0 else None,
                                                ev_exp[jc // 2] if dc == 0 else None,
                                                ev_lsb if (jc == 0 and dc == 0) else None,
                                                ev_bank_free[dc] if jc == 0 else None),
                                   inc=("pe" if dc == 7 else None))
                        ev_vt_free[k] = ev_pvl
                    for dc in range(8):
                        ev_oa_ev[db * 8 + dc] = P.emit(
                            "vector",
                            lambda e, db=db, dc=dc: e.tensor_tensor(
                                oa_sb[:, db * 8 + dc, :], bank[dc][:, :], linv_sb[:, :], ALU.mult),
                            waits=_maxev(ev_pvl,
                                         state["a_pe"] if (hf == 0 and db == 0 and dc == 0) else None),
                            inc="dv")
                        ev_bank_free[dc] = ev_oa_ev[db * 8 + dc]
                ev_oa = _maxev(*ev_oa_ev)
                woT_src = woT.rearrange("(c p) o -> p c o", c=16, p=128)
                ev_womm = None
                for oB in range(2):
                    for dc2 in range(16):
                        k = (oB * 16 + dc2) % 3
                        ev_wol = P.dma("gpsimd", f"wo{k}", wo_sb[k][:, :],
                                       woT_src[:, dc2, oB * 1024:(oB + 1) * 1024],
                                       waits=_maxev(ev_wo_free[k],
                                                    state["a_pe"] if (hf == 0 and oB == 0 and dc2 < 3) else None))
                        for oc in range(8):
                            ev_womm = P.emit("tensor",
                                   lambda e, oc=oc, k=k, dc2=dc2: e.matmul(
                                       bank[oc][:, :],
                                       wo_sb[k][:, oc * 128:(oc + 1) * 128],
                                       oa_sb[:, dc2, :],
                                       start=(dc2 == 0), stop=(dc2 == 15)),
                                   waits=_maxev(ev_wol if oc == 0 else None,
                                                ev_oa_ev[dc2] if oc == 0 else None,
                                                ev_bank_free[oc] if dc2 == 0 else None),
                                   inc=("pe" if oc == 7 else None))
                        ev_wo_free[k] = ev_womm
                    for op2 in range(4):
                        oo = oB * 8 + 2 * op2
                        k2 = op2 % 2
                        ev_xl = P.dma("sync", f"xr{k2}", xr_sb[k2][:, :],
                                      xH[hf, :, oo * TT:(oo + 2) * TT],
                                      waits=_maxev(ev_xr_free[k2],
                                                   state["a_pe"] if (hf == 0 and oB == 0 and op2 < 2) else None))
                        ev_x2c = P.emit("vector",
                                        lambda e, oo=oo, k2=k2, op2=op2, i0=i0: e.tensor_tensor(
                                            x2[:, oo:oo + 2, i0:i0 + TT],
                                            xr_sb[k2][:, :].rearrange("p (j t) -> p j t", j=2),
                                            pair[op2][:, :].rearrange("p (j t) -> p j t", j=2),
                                            ALU.add),
                                        waits=_maxev(ev_xl, ev_womm), inc="dv")
                        ev_xr_free[k2] = ev_x2c
                        ev_bank_free[2 * op2] = ev_x2c
                        ev_bank_free[2 * op2 + 1] = ev_x2c
                if hf == 0:
                    ev_x2_half0 = P.now("dv")
                else:
                    state["b_wo_pe"] = P.now("pe")
                    if "C" in phases:
                        c_half(1, ev_bank_free, P.now("dv"))
                ev_pt_free = _maxev(P.now("pe"),
                                    P.now("dbg") if (debug and "dbg" in P.counts) else None)
            state["b_x2"] = P.now("dv")
            state["b_pe"] = P.now("pe")
            state["b_ac"] = P.now("ac")
            if debug:
                ev = state["b_x2"]
                for oo in range(16):
                    P.dma("gpsimd", "dbg", dbg["x2"][oo * 128:(oo + 1) * 128, :], x2[:, oo, :], waits=_maxev(ev))

        cctx = {"prev_bc": None, "prev_rstd": None, "y2_ev": [None, None]}

        def c_half(tt, ev_bank_free, ev_x2_half):
            ev_cast_pe = [None, None]
            ev_sq_pe = [None, None]
            ev_sq = [None] * 16
            ev_sums = None
            for c in range(16):
                ev_cast = P.emit("vector",
                                 lambda e, c=c, tt=tt: e.tensor_copy(cb2[c % 2][:, :], x2[:, c, tt * TT:(tt + 1) * TT]),
                                 waits=_maxev(ev_x2_half if c == 0 else None,
                                              state["a_vst"] if (tt == 0 and c == 0) else None,
                                              ev_cast_pe[c % 2], ev_sq[c - 2] if c >= 2 else None),
                                 inc="dv")
                ev_sq[c] = P.emit("scalar",
                                  lambda e, c=c: e.activation(xsq[c % 2][:, :], cb2[c % 2][:, :], AF.Square),
                                  waits=_maxev(ev_cast, ev_sq_pe[c % 2]),
                                  inc="ac")
                ev_cast_pe[c % 2] = P.emit(
                    "tensor",
                    lambda e, c=c: e.matmul(bank[0][0:1, :], ones_bf[:, :], cb2[c % 2][:, :],
                                            start=(c == 0), stop=(c == 15)),
                    waits=_maxev(ev_cast, ev_bank_free[0] if c == 0 else None), inc="pe")
                ev_sq_pe[c % 2] = P.emit(
                    "tensor",
                    lambda e, c=c: e.matmul(bank[1][0:1, :], ones_bf[:, :], xsq[c % 2][:, :],
                                            start=(c == 0), stop=(c == 15)),
                    waits=_maxev(ev_sq[c], ev_bank_free[1] if c == 0 else None), inc="pe")
                ev_sums = ev_sq_pe[c % 2]
            P.emit("vector", lambda e: e.tensor_scalar_mul(m_row[:, :], bank[0][0:1, :], 1.0 / D),
                   waits=_maxev(ev_sums))
            ev_m = P.emit("vector", lambda e: e.tensor_mul(var_row[:, :], m_row[:, :], m_row[:, :]),
                          waits=_maxev(cctx["prev_rstd"]), inc="dv")
            ev_bank_free[0] = ev_m
            ev_var = P.emit("vector",
                            lambda e: e.scalar_tensor_tensor(var_row[:, :], bank[1][0:1, :], 1.0 / D,
                                                             var_row[:, :], ALU.mult, ALU.subtract),
                            inc="dv")
            ev_bank_free[1] = ev_var
            P.emit("scalar", lambda e: e.activation(var_row[:, :], var_row[:, :], AF.Ln, bias=EPS),
                   waits=_maxev(ev_var, cctx["prev_bc"]))
            ev_rstd = P.emit("scalar", lambda e: e.activation(rstd_row[:, :], var_row[:, :], AF.Exp, scale=-0.5),
                             inc="ac")
            cctx["prev_rstd"] = ev_rstd
            ev_mr = P.emit("vector", lambda e: e.tensor_mul(mr_row[:, :], m_row[:, :], rstd_row[:, :]),
                           waits=_maxev(ev_rstd, cctx["prev_bc"]), inc="dv")
            ev_bc_r = P.emit("tensor",
                             lambda e: e.matmul(bank[2][:, :], ones1f[:, :], rstd_row[:, :], start=True, stop=True),
                             waits=_maxev(ev_rstd, ev_bank_free[2]), inc="pe")
            ev_bc_mr = P.emit("tensor",
                              lambda e: e.matmul(bank[3][:, :], ones1f[:, :], mr_row[:, :], start=True, stop=True),
                              waits=_maxev(ev_mr, ev_bank_free[3]), inc="pe")
            cctx["prev_bc"] = _maxev(ev_bc_r, ev_bc_mr)
            ev_cp_r = P.emit("vector", lambda e: e.tensor_copy(rb2_sb[:, :], bank[2][:, :]),
                             waits=_maxev(ev_bc_r), inc="dv")
            ev_cp_m = P.emit("vector", lambda e: e.tensor_copy(mb2_sb[:, :], bank[3][:, :]),
                             waits=_maxev(ev_bc_mr), inc="dv")
            ev_bank_free[2] = ev_cp_r
            ev_bank_free[3] = ev_cp_m
            rb = rb2_sb[:, :].unsqueeze(1).broadcast_to([128, 16, TT])
            mb = mb2_sb[:, :].unsqueeze(1).broadcast_to([128, 16, TT])
            y2v = y2h[:, :, :] if tt == 0 else y2_sb[:, :, TT:2 * TT]
            P.emit("vector",
                   lambda e, tt=tt, rb=rb, y2v=y2v: e.tensor_tensor(
                       y2v, x2[:, :, tt * TT:(tt + 1) * TT], rb, ALU.mult),
                   waits=_maxev((P.now("dbg") if (debug and "dbg" in P.counts) else None)
                                if tt == 1 else None))
            last = P.emit("vector",
                          lambda e, y2v=y2v, mb=mb: e.tensor_tensor(y2v, y2v, mb, ALU.subtract),
                          inc="dv")
            cctx["y2_ev"][tt] = last

        def phase_c():
            state["c_y2_tt"] = [cctx["y2_ev"][0], cctx["y2_ev"][1]]
            state["c_y2"] = _maxev(cctx["y2_ev"][0], cctx["y2_ev"][1])
            state["c_pe"] = P.now("pe")
            if debug:
                ev = state["c_y2"]
                for c in range(16):
                    P.dma("gpsimd", "dbg", dbg["y2"][c * 128:(c + 1) * 128, :], y2_sb[:, c, :], waits=_maxev(ev))

        def phase_d():
            ev_bank_free = [None] * 8
            ev_w_free = [None, None]
            ev_sg_free = [None] * 4
            ev_h1_free = [None, None]
            ev_y2 = state["c_y2"]
            pend = []

            def emit_h128(a_prev, ev_hids):
                kb = a_prev % 2
                ks0 = 2 * (a_prev % 2)
                ev_ev = [None, None]
                for tt in range(2):
                    bh = 4 + 2 * kb + tt
                    ev_h128 = P.emit("tensor",
                           lambda e, bh=bh, ks=ks0 + tt: e.matmul(bank[bh][:, :], h128[:, :], sg_sb[ks][:, :],
                                                                  start=True, stop=True),
                           waits=_maxev(ev_hids[tt], ev_bank_free[bh]), inc="pe")
                    ev_sg_free[ks0 + tt] = ev_h128
                    ev_ev[tt] = P.emit("scalar",
                                       lambda e, bh=bh, kb=kb, tt=tt: e.copy(
                                           h1_sb[kb][:, tt * TT:(tt + 1) * TT], bank[bh][:, :]),
                                       waits=_maxev(ev_h128, ev_h1_free[kb] if tt == 0 else None), inc="ac")
                    ev_bank_free[bh] = ev_ev[tt]
                h1_dst = h1_dram.rearrange("(b s a) t -> s b (a t)", b=64, s=2, a=64)
                ev_h1st = P.dma("sync", f"h1_{kb}",
                                h1_dst[:, :, a_prev * OWN:(a_prev + 1) * OWN],
                                h1_sb[kb][:, :], waits=_maxev(ev_ev[0], ev_ev[1]))
                ev_h1_free[kb] = ev_h1st

            for a2 in range(32):
                k = a2 % 2
                ev_w1 = P.dma("gpsimd", f"w1_{k}", w1_sb[k][:, :, :].rearrange("p c t -> p (c t)"),
                              w1H[a2, :, :],
                              waits=_maxev(ev_w_free[k],
                                           state["b_wo_pe"] if a2 < 2 else None,
                                           (P.now("dbg") if (debug and "dbg" in P.counts) else None)
                                           if a2 < 2 else None,
                                           [P.now("h1_0"), P.now("h1_1")] if a2 < 2 else None))
                ev_w3 = P.dma("gpsimd", f"w3_{k}", w3_sb[k][:, :, :].rearrange("p c t -> p (c t)"),
                              w3H[a2, :, :],
                              waits=_maxev(state["b_wo_pe"] if a2 < 2 else None,
                                           state["b_x2"] if a2 < 2 else None))
                for j in range(2):
                    a = 2 * a2 + j
                    ks0 = 2 * (a % 2)
                    order = ([(c, tt) for tt in range(2) for c in range(16)]
                             if a == 0 else
                             [(c, tt) for c in range(16) for tt in range(2)])
                    def y2rhs(c, tt):
                        return y2h[:, c, :] if tt == 0 else y2_sb[:, c, TT:2 * TT]
                    ev_g = None
                    for c, tt in order:
                        ev_g = P.emit("tensor",
                               lambda e, c=c, k=k, j=j, tt=tt: e.matmul(
                                   bank[tt][:, :], w1_sb[k][:, c, j * 128:(j + 1) * 128],
                                   y2rhs(c, tt),
                                   start=(c == 0), stop=(c == 15)),
                               waits=_maxev(ev_w1 if (c == 0 and tt == 0) else None,
                                            state["c_y2_tt"][tt] if (a == 0 and c == 0) else None,
                                            _maxev(ev_bank_free[0], ev_bank_free[1])
                                            if (c == 0 and tt == 0) else None),
                               inc=("pe" if (c == 15 and tt == 1) else None))
                    if pend:
                        emit_h128(*pend.pop())
                    ev_lm = None
                    for c, tt in order:
                        ev_lm = P.emit("tensor",
                               lambda e, c=c, k=k, j=j, tt=tt: e.matmul(
                                   bank[2 + tt][:, :], w3_sb[k][:, c, j * 128:(j + 1) * 128],
                                   y2rhs(c, tt),
                                   start=(c == 0), stop=(c == 15)),
                               waits=_maxev(ev_w3 if (c == 0 and tt == 0) else None,
                                            _maxev(ev_bank_free[2], ev_bank_free[3])
                                            if (c == 0 and tt == 0) else None),
                               inc=("pe" if (c == 15 and tt == 1) else None))
                    ev_hids = [None, None]
                    for tt in range(2):
                        ev_silu = P.emit("scalar",
                                         lambda e, tt=tt, ks=ks0 + tt: e.activation(sg_sb[ks][:, :], bank[tt][:, :], AF.Silu),
                                         waits=_maxev(ev_g, ev_sg_free[ks0 + tt]), inc="ac")
                        if tt == 1:
                            ev_bank_free[0] = ev_silu
                            ev_bank_free[1] = ev_silu
                        ev_hids[tt] = P.emit("vector",
                                             lambda e, tt=tt, ks=ks0 + tt: e.tensor_tensor(
                                                 sg_sb[ks][:, :], sg_sb[ks][:, :], bank[2 + tt][:, :], ALU.mult),
                                             waits=_maxev(ev_silu, ev_lm), inc="dv")
                        if tt == 1:
                            ev_bank_free[2] = ev_hids[1]
                            ev_bank_free[3] = ev_hids[1]
                    pend.append((a, ev_hids))
                ev_w_free[k] = _maxev(P.now("pe"))
            emit_h128(*pend.pop())
            state["d_h1st"] = [P.now("h1_0"), P.now("h1_1")]
            state["d_ac"] = P.now("ac")
            state["d_dv"] = P.now("dv")
            state["d_pe"] = P.now("pe")

        def phase_e():
            h1_src = h1_dram.rearrange("(b p) t -> p b t", p=128)
            h2_src = h2_dram.rearrange("(b p) t -> p b t", p=128)
            ev_hp_free = [None, None, None]
            ev_h2_free = [None, None, None, None]
            ev_bank_free = [None] * 8
            ev_h2st = [None] * 32
            for bp in range(32):
                k = bp % 3
                ev_l = P.dma("sync", f"hp{k}", hp_sb[k][:, :],
                             h1_src[:, 2 * bp:2 * bp + 2, :],
                             waits=_maxev(state["d_h1st"] if bp < 3 else None,
                                          state["d_pe"] if bp < 3 else None,
                                          ev_hp_free[k]))
                for bj in range(2):
                    b = 2 * bp + bj
                    kb = b % 4
                    pp = b % 4
                    ev_mm = None
                    for half in range(2):
                        ev_mm = P.emit("tensor",
                               lambda e, pp=pp, half=half, k=k, bj=bj: e.matmul(
                                   bank[2 * pp + half][:, :], h64p[:, :],
                                   hp_sb[k][:, bj * 1024 + half * TT: bj * 1024 + (half + 1) * TT],
                                   start=True, stop=True),
                               waits=_maxev(ev_l if half == 0 else None,
                                            _maxev(ev_bank_free[2 * pp], ev_bank_free[2 * pp + 1])
                                            if half == 0 else None,
                                            _maxev(state["d_ac"], state["d_dv"]) if b < 4 else None),
                               inc=("pe" if half == 1 else None))
                    if bj == 1:
                        ev_hp_free[k] = ev_mm
                    ev_ev = P.emit("scalar",
                                   lambda e, pp=pp, kb=kb: e.activation(
                                       h2_sb[kb][:, :], pair[pp][:, :], AF.Copy, scale=FWHT_SCALE),
                                   waits=_maxev(ev_mm, ev_h2_free[kb]), inc="ac")
                    ev_bank_free[2 * pp] = ev_ev
                    ev_bank_free[2 * pp + 1] = ev_ev
                    ev_st = P.dma("scalar", f"h2_{kb}",
                                  h2_dram[b * 128:(b + 1) * 128, :], h2_sb[kb][:, :],
                                  waits=_maxev(ev_ev))
                    if bj == 0:
                        ev_h2st[bp] = [ev_st]
                    else:
                        ev_h2st[bp] = ev_h2st[bp] + [ev_st]
                    ev_h2_free[kb] = ev_st
            ev_h2st_all = [P.now(f"h2_{i}") for i in range(4)]
            ev_w2_free = [None, None, None]
            ev_x3_free = [None, None]
            for oB in range(4):
                for bp in range(32):
                    k = bp % 3
                    ev_wl = P.dma("gpsimd", f"w2_{k}", w2_sb[k][:, :],
                                  w2H[oB, bp, :, :],
                                  waits=_maxev(ev_w2_free[k],
                                               state["d_pe"] if (oB == 0 and bp < 3) else None,
                                               state["b_x2"] if (oB == 0 and bp < 3) else None))
                    ev_hl = P.dma("sync", f"hp{k}", hp_sb[k][:, :],
                                  h2_src[:, 2 * bp:2 * bp + 2, :],
                                  waits=_maxev(ev_h2st[bp] if oB == 0 else None,
                                               ev_h2st_all if (oB == 0 and bp < 3) else None,
                                               ev_hp_free[k]))
                    evp = None
                    for bj in range(2):
                        for oc in range(4):
                            for tt in range(2):
                                bk = oc * 2 + tt
                                evp = P.emit("tensor",
                                       lambda e, bk=bk, oc=oc, tt=tt, k=k, bj=bj, bp=bp: e.matmul(
                                           bank[bk][:, :],
                                           w2_sb[k][:, bj * 512 + oc * 128: bj * 512 + (oc + 1) * 128],
                                           hp_sb[k][:, bj * 1024 + tt * TT: bj * 1024 + (tt + 1) * TT],
                                           start=(bp == 0 and bj == 0), stop=(bp == 31 and bj == 1)),
                                       waits=_maxev(ev_wl if (bj == 0 and oc == 0 and tt == 0) else None,
                                                    ev_hl if (bj == 0 and oc == 0 and tt == 0) else None,
                                                    ev_bank_free[bk]
                                                    if (bp == 0 and bj == 0) else None),
                                       inc=("pe" if (bj == 1 and tt == 1 and (bp == 31 or oc == 3)) else None))
                    if bp == 31:
                        for oc2 in range(4):
                            state[f"e2_stop{oc2}"] = ("pe", P.counts["pe"] - (3 - oc2))
                    ev_w2_free[k] = evp
                    ev_hp_free[k] = evp
                for oc in range(4):
                    oo = oB * 4 + oc
                    k3 = oc % 2
                    ev_add = P.emit("vector",
                                    lambda e, oc=oc, oo=oo, k3=k3: e.tensor_tensor(
                                        x3_sb[k3][:, :], x2[:, oo, :], pair[oc][:, :], ALU.add),
                                    waits=_maxev(state[f"e2_stop{oc}"], ev_x3_free[k3],
                                                 state["b_x2"] if (oB == 0 and oc == 0) else None),
                                    inc="dv")
                    ev_bank_free[2 * oc] = ev_add
                    ev_bank_free[2 * oc + 1] = ev_add
                    ev_out = P.dma("scalar", f"x3_{k3}",
                                   outT[oo * 128:(oo + 1) * 128, :],
                                   x3_sb[k3][:, :], waits=_maxev(ev_add))
                    ev_x3_free[k3] = ev_out
            state["out_done"] = [P.now("x3_0"), P.now("x3_1")]

        if "A" in phases:
            phase_a()
        if "B" in phases:
            phase_b()
        if "C" in phases:
            phase_c()
        if "D" in phases:
            phase_d()
        if "E" in phases:
            phase_e()

        if "E" in phases:
            P.emit("sync", lambda e: e.nop(),
                   waits=_maxev(state["out_done"],
                                (P.now("dbg") if "dbg" in P.counts else None) if debug else None))
        else:
            ev = P.emit("gpsimd", lambda e: e.memset(x3_sb[0][:, :], 0.0),
                        waits=_maxev(state.get("b_x2"), state.get("c_y2"),
                                     state.get("a_vst"), P.now("dv"), P.now("ac"),
                                     P.now("pe")),
                        inc="gp")
            for oo in range(16):
                P.dma("sync", "ph", outT[oo * 128:(oo + 1) * 128, :],
                      x3_sb[0][:, :], waits=_maxev(ev))
            P.emit("sync", lambda e: e.nop(),
                   waits=_maxev(P.now("ph"),
                                (P.now("dbg") if "dbg" in P.counts else None) if debug else None))

        with nc.Block() as block:
            P.finalize_block(block)

    return nc



def _hadamard(n):
    h = np.array([[1.0]], dtype=np.float32)
    while h.shape[0] < n:
        h = np.block([[h, h], [h, -h]])
    return h


def prep_inputs(L0_attn_hidden_states, L0_attn_Wq, L0_attn_Wk, L0_attn_Wv,
                L0_attn_Wo, L0_ffn_W1, L0_ffn_W2, L0_ffn_W3):
    x = np.asarray(L0_attn_hidden_states)
    w1T = np.asarray(L0_ffn_W1).T.astype(np.float32)
    w3T = np.asarray(L0_ffn_W3).T.astype(np.float32)
    w2T = np.asarray(L0_ffn_W2).T.astype(np.float32)
    wvT = np.asarray(L0_attn_Wv).T.astype(np.float32)
    base = {
        "wqkT": np.ascontiguousarray(
            np.concatenate([np.asarray(L0_attn_Wq), np.asarray(L0_attn_Wk)], axis=0).T).astype(np.float32),
        "woT": np.ascontiguousarray(np.asarray(L0_attn_Wo).T).astype(np.float32),
        "wvH": np.ascontiguousarray(
            wvT.reshape(16, 128, D).transpose(1, 0, 2).reshape(128, 16 * D)),
        "w1H": np.ascontiguousarray(
            w1T.reshape(16, 128, 32, 256).transpose(2, 1, 0, 3).reshape(32, 128, 16 * 256)),
        "w3H": np.ascontiguousarray(
            w3T.reshape(16, 128, 32, 256).transpose(2, 1, 0, 3).reshape(32, 128, 16 * 256)),
        "w2H": np.ascontiguousarray(
            w2T.reshape(64, 2, 32, 2, 4, 512).transpose(4, 2, 1, 0, 3, 5).reshape(4, 32, 128, 1024)),
        "idt": np.eye(128, dtype=ml_dtypes.bfloat16),
        "h128": _hadamard(128).astype(ml_dtypes.bfloat16),
    }
    h64 = _hadamard(64)
    h64p = np.zeros((128, 128), dtype=np.float32)
    h64p[0:64, 0:64] = h64
    h64p[64:128, 64:128] = h64
    base["h64p"] = h64p.astype(ml_dtypes.bfloat16)
    in_maps = []
    for core in range(8):
        b, r = core // 4, core % 4
        m = dict(base)
        xb = np.roll(x[b].T, -r * OWN, axis=1).astype(np.float32)
        m["xH"] = np.ascontiguousarray(
            xb.reshape(16, 128, 8, 512).transpose(2, 1, 0, 3).reshape(8, 128, 16 * 512))
        in_maps.append(m)
    return in_maps


def kernel(**inputs):
    in_maps = prep_inputs(**inputs)
    nc = build_program(debug=False)
    res = run_bass_kernel_spmd(nc, in_maps, core_ids=list(range(8)))
    out = np.empty((2, S, D), dtype=np.float32)
    for core in range(8):
        b, r = core // 4, core % 4
        out[b, r * OWN:(r + 1) * OWN, :] = res.results[core]["outT"].T
    return out


# version 19 (final)
# speedup vs baseline: 1.0465x; 1.0061x over previous
import numpy as np
import ml_dtypes
import concourse.bass as bass
import concourse.mybir as mybir
from concourse.bass_utils import run_bass_kernel_spmd
from contextlib import ExitStack

F32 = mybir.dt.float32
BF16 = mybir.dt.bfloat16
U8 = mybir.dt.uint8
AF = mybir.ActivationFunctionType
ALU = mybir.AluOpType
AX = mybir.AxisListType.X

D = 2048
S = 4096
HD = 8192
TT = 512
NTT = S // TT
OWN = 1024
EPS = 1e-5
FWHT_SCALE = 1.0 / float(np.sqrt(HD))

ENGINES = ("tensor", "vector", "scalar", "gpsimd", "sync")


class Plan:

    def __init__(self, nc):
        self.nc = nc
        self.ops = {e: [] for e in ENGINES}
        self.sem_handles = {}
        self.counts = {}

    AUTO_CHAIN = {"vector": "dv", "scalar": "ac", "gpsimd": "gp"}

    def init_sems(self, es):
        self._es = es
        for name in ("pe", "dv", "ac", "gp", "sy"):
            self.sem_handles[name] = es.enter_context(self.nc.semaphore(name))
            self.counts[name] = 0
        self.last_chain = {}

    def sem(self, name):
        if name not in self.sem_handles:
            self.sem_handles[name] = self._es.enter_context(self.nc.semaphore(name))
            self.counts[name] = 0
        return name

    def emit(self, engine, fn, waits=(), inc=None, inc_amt=1, chain=True):
        wl = [(s, v) for (s, v) in waits if s is not None]
        chained = chain and engine in self.AUTO_CHAIN
        if chained:
            own = self.AUTO_CHAIN[engine]
            if inc is None:
                inc, inc_amt = own, 1
            prev = self.last_chain.get(engine)
            if prev is not None:
                wl.append(prev)
        ev = None
        if inc is not None:
            self.counts[inc] += inc_amt
            ev = (inc, self.counts[inc])
        if chained and inc == self.AUTO_CHAIN[engine]:
            self.last_chain[engine] = ev
        self.ops[engine].append((tuple(wl), fn, inc, inc_amt))
        return ev

    def dma(self, engine, stream, out, in_, waits=(), **kw):
        q = self.sem(stream)
        return self.emit(engine,
                         lambda e, out=out, in_=in_, kw=kw: e.dma_start(out=out, in_=in_, **kw),
                         waits=waits, inc=q, inc_amt=16, chain=False)

    def now(self, sem):
        return (sem, self.counts.get(sem, 0))

    def replay(self, engine, eng):
        sems = self.sem_handles
        waited = {}
        for wl, fn, inc, inc_amt in self.ops[engine]:
            for s, v in sorted(wl):
                if waited.get(s, 0) >= v:
                    continue
                eng.wait_ge(sems[s], v)
                waited[s] = v
            inst = fn(eng)
            if inc is not None and inst is not None:
                inst.then_inc(sems[inc], inc_amt)

    def finalize_block(self, block):
        for engine in ENGINES:
            meth = getattr(block, engine)

            def body(eng, engine=engine):
                self.replay(engine, eng)
            meth(body)


def _maxev(*evs):
    best = {}

    def add(ev):
        if ev is None:
            return
        if isinstance(ev, list):
            for e2 in ev:
                add(e2)
            return
        s, v = ev
        if best.get(s, -1) < v:
            best[s] = v
    for ev in evs:
        add(ev)
    return [(s, v) for s, v in best.items()]


def build_program(debug=False, phases="ABCDE"):
    nc = bass.Bass("TRN2", target_bir_lowering=False)

    xH = nc.dram_tensor("xH", [NTT, 128, 16 * TT], F32, kind="ExternalInput").ap()
    wqkT = nc.dram_tensor("wqkT", [D, 128], F32, kind="ExternalInput").ap()
    wvH = nc.dram_tensor("wvH", [128, 16 * D], F32, kind="ExternalInput").ap()
    woT = nc.dram_tensor("woT", [D, D], F32, kind="ExternalInput").ap()
    w1H = nc.dram_tensor("w1H", [32, 128, 16 * 256], F32, kind="ExternalInput").ap()
    w3H = nc.dram_tensor("w3H", [32, 128, 16 * 256], F32, kind="ExternalInput").ap()
    w2H = nc.dram_tensor("w2H", [4, 32, 128, 1024], F32, kind="ExternalInput").ap()
    h128_in = nc.dram_tensor("h128", [128, 128], BF16, kind="ExternalInput").ap()
    h64p_in = nc.dram_tensor("h64p", [128, 128], BF16, kind="ExternalInput").ap()
    idt_in = nc.dram_tensor("idt", [128, 128], BF16, kind="ExternalInput").ap()
    outT = nc.dram_tensor("outT", [D, OWN], F32, kind="ExternalOutput").ap()

    scratch_kind = "ExternalOutput" if debug else "Internal"
    v_dram = nc.dram_tensor("v_scratch", [S, D], BF16, kind=scratch_kind).ap()
    h1_dram = nc.dram_tensor("h1_scratch", [HD, OWN], BF16, kind=scratch_kind).ap()
    h2_dram = nc.dram_tensor("h2_scratch", [HD, OWN], BF16, kind=scratch_kind).ap()

    dbg = {}
    if debug:
        dbg["kf"] = nc.dram_tensor("dbg_kf", [72, S], BF16, kind="ExternalOutput").ap()
        dbg["qf"] = nc.dram_tensor("dbg_qf", [72, OWN], BF16, kind="ExternalOutput").ap()
        dbg["pt"] = nc.dram_tensor("dbg_pt", [S, OWN], BF16, kind="ExternalOutput").ap()
        dbg["x2"] = nc.dram_tensor("dbg_x2", [D, OWN], F32, kind="ExternalOutput").ap()
        dbg["y2"] = nc.dram_tensor("dbg_y2", [D, OWN], BF16, kind="ExternalOutput").ap()

    P = Plan(nc)
    es = ExitStack()
    with es:
        P.init_sems(es)

        def sb(name, shape, dtype):
            return es.enter_context(nc.sbuf_tensor(name, shape, dtype))

        ones_bf = sb("ones_bf", [128, 1], BF16)
        ones1f = sb("ones1f", [1, 128], F32)
        idt = sb("idt_t", [128, 128], BF16)
        h128 = sb("h128_t", [128, 128], BF16)
        h64p = sb("h64p_t", [128, 128], BF16)
        wqk = sb("wqk", [128, 16, 128], BF16)
        kf_aug = sb("kf_aug", [72, S], BF16)
        qf_aug = sb("qf_aug", [72, OWN], BF16)
        m_row = sb("m_row", [1, TT], F32)
        var_row = sb("var_row", [1, TT], F32)
        rstd_row = sb("rstd_row", [1, TT], F32)
        mr_row = sb("mr_row", [1, TT], F32)
        linv_row = sb("linv_row", [1, TT], F32)
        linv_sb = sb("linv_sb", [128, TT], F32)
        mx_parts = sb("mx_parts", [128, 64], F32)
        m_col = sb("m_col", [128, 1], F32)
        m_bf = sb("m_bf", [128, 1], BF16)
        xsq = [sb(f"xsq{i}", [128, TT], BF16) for i in range(2)]
        cb2 = [sb(f"cb2_{i}", [128, TT], BF16) for i in range(2)]
        v_sb = [sb(f"v_sb{i}", [128, TT], BF16) for i in range(3)]
        sg_sb = [sb(f"sg{i}", [128, TT], BF16) for i in range(4)]
        h1_sb = [sb(f"h1s{i}", [128, 1024], BF16) for i in range(2)]
        h2_sb = [sb(f"h2s{i}", [128, 1024], BF16) for i in range(4)]
        y2h = sb("y2h", [128, 16, TT], BF16)
        rb2_sb = sb("rb2_sb", [128, TT], F32)
        mb2_sb = sb("mb2_sb", [128, TT], F32)
        xr_sb = [sb(f"xr{i}", [128, 1024], F32) for i in range(2)]

        ar0 = sb("arena0", [128, 65536], U8)
        ar1 = sb("arena1", [128, 32768], U8)
        ar2 = sb("arena2", [128, 32768], U8)

        def carve(ar, off, dtype, shape):
            nb = int(np.prod(shape[1:])) * mybir.dt.size(dtype)
            ap = ar[:, off:off + nb].bitcast(dtype)
            if len(shape) == 3:
                ap = ap.rearrange("p (c t) -> p c t", c=shape[1], t=shape[2])
            return ap

        wv_r = carve(ar0, 0, BF16, [128, 16, D])
        x2 = carve(ar0, 0, F32, [128, 16, OWN])
        xs_r = [carve(ar1, i * 16384, BF16, [128, 16, TT]) for i in range(2)]
        pt_sb = carve(ar1, 0, BF16, [128, 32, TT])
        y2_sb = carve(ar1, 0, BF16, [128, 16, OWN])
        y1_r = [carve(ar2, i * 16384, BF16, [128, 16, TT]) for i in range(2)]
        oa_sb = carve(ar2, 0, BF16, [128, 16, TT])
        vt_sb = [carve(ar2, 16384 + i * 2048, BF16, [128, 1024]) for i in range(4)]
        wo_sb = [carve(ar2, 24576 + i * 2048, BF16, [128, 1024]) for i in range(3)]
        w1_sb = [carve(ar2, i * 8192, BF16, [128, 16, 256]) for i in range(2)]
        w3_sb = [carve(ar2, 16384 + i * 8192, BF16, [128, 16, 256]) for i in range(2)]

        hp_sb = [carve(ar2, 0 + i * 4096, BF16, [128, 2048]) for i in range(3)]

        w2_sb = [carve(ar2, 16384 + i * 2048, BF16, [128, 1024]) for i in range(3)]
        x3_sb = [carve(ar2, 22528 + i * 4096, F32, [128, 1024]) for i in range(2)]

        pair = [es.enter_context(nc.psum_tensor(f"pp{i}", [128, 1024], F32))
                for i in range(4)]
        bank = [pair[i // 2][:, (i % 2) * TT:(i % 2 + 1) * TT] for i in range(8)]

        for i, v in enumerate((EPS, -0.5, FWHT_SCALE)):
            ct = sb(f"const{i}", [128, 1], F32)
            P.emit("gpsimd", lambda e, ct=ct, v=v: e.memset(ct[:, :], float(v)))
            nc.const_aps.register(F32, float(v), ct[:, :])
        P.emit("gpsimd", lambda e: e.memset(ones_bf[:, :], 1.0))
        P.emit("gpsimd", lambda e: e.memset(ones1f[:, :], 1.0))
        P.emit("gpsimd", lambda e: e.memset(kf_aug[64:65, :], 1.0))
        ev_memset = P.now("gp")
        ev_x0 = P.dma("gpsimd", "x0", xs_r[0][:, :, :].rearrange("p c t -> p (c t)"), xH[0, :, :])
        wqk_src = wqkT.rearrange("(c p) q -> p c q", c=16, p=128)
        ev_wqk = P.dma("gpsimd", "wqk", wqk[:, :, :], wqk_src)
        ev_x1 = P.dma("gpsimd", "x1", xs_r[1][:, :, :].rearrange("p c t -> p (c t)"), xH[1, :, :])
        P.dma("gpsimd", "init", idt[:, :], idt_in[:, :])
        P.dma("gpsimd", "init", h128[:, :], h128_in[:, :])
        P.dma("gpsimd", "init", h64p[:, :], h64p_in[:, :])
        wv_src = wvH.rearrange("p (c o) -> p c o", c=16, o=D)
        ev_wv = None
        for h in range(4):
            ev_wv = P.dma("gpsimd", "wv", wv_r[:, 4 * h:4 * h + 4, :], wv_src[:, 4 * h:4 * h + 4, :])
        ev_init3 = P.now("init")

        state = {}

        def phase_a():
            ev_bank_free = [None] * 8
            ev_xs_free = [None, None]
            ev_y1_free = [None, None]
            ev_xsq_free = [None, None]
            ev_vsb_free = [None, None, None]
            ctx = {"prev_bc_r": None, "prev_bc_mr": None, "prev_rstd": None,
                   "ev_apply": [None] * NTT, "ev_ld": [None] * NTT}
            ev_kf = [None] * NTT

            def load_x(tau):
                if tau == 0:
                    ctx["ev_ld"][0] = ev_x0
                    return
                if tau == 1:
                    ctx["ev_ld"][1] = ev_x1
                    return
                xs = xs_r[tau % 2]
                ctx["ev_ld"][tau] = P.dma(
                    "gpsimd", f"x{tau % 2}", xs[:, :, :].rearrange("p c t -> p (c t)"),
                    xH[tau, :, :], waits=_maxev(ev_xs_free[tau % 2]))

            def stats(tau):
                xs = xs_r[tau % 2]
                ev_ld = ctx["ev_ld"][tau]
                ev_sum = None
                ev_sumsq = None
                ev_sq = [None] * 16
                for c in range(16):
                    ev_sq[c] = P.emit(
                        "scalar",
                        lambda e, c=c, xs=xs: e.activation(xsq[c % 2][:, :], xs[:, c, :], AF.Square),
                        waits=_maxev(ev_ld, ev_xsq_free[c % 2]), inc="ac")
                    ev_sum = P.emit("tensor",
                                    lambda e, c=c, xs=xs: e.matmul(bank[0][0:1, :], ones_bf[:, :], xs[:, c, :],
                                                                   start=(c == 0), stop=(c == 15)),
                                    waits=_maxev(ev_ld if c == 0 else None,
                                                 ev_bank_free[0] if c == 0 else None,
                                                 ev_memset if (tau == 0 and c == 0) else None,
                                                 ev_wqk if (tau == 0 and c == 0) else None,
                                                 ev_init3 if (tau == 0 and c == 0) else None),
                                    inc="pe")
                    ev_sumsq = P.emit("tensor",
                                      lambda e, c=c: e.matmul(bank[1][0:1, :], ones_bf[:, :], xsq[c % 2][:, :],
                                                              start=(c == 0), stop=(c == 15)),
                                      waits=_maxev(ev_sq[c], ev_bank_free[1] if c == 0 else None),
                                      inc="pe")
                    ev_xsq_free[c % 2] = ev_sumsq
                P.emit("vector", lambda e: e.tensor_scalar_mul(m_row[:, :], bank[0][0:1, :], 1.0 / D),
                       waits=_maxev(ev_sum))
                ev_m = P.emit("vector", lambda e: e.tensor_mul(var_row[:, :], m_row[:, :], m_row[:, :]),
                              waits=_maxev(ctx["prev_rstd"]), inc="dv")
                ev_bank_free[0] = ev_m
                ev_var = P.emit("vector",
                                lambda e: e.scalar_tensor_tensor(var_row[:, :], bank[1][0:1, :], 1.0 / D,
                                                                 var_row[:, :], ALU.mult, ALU.subtract),
                                waits=_maxev(ev_sumsq), inc="dv")
                ev_bank_free[1] = ev_var
                P.emit("scalar", lambda e: e.activation(var_row[:, :], var_row[:, :], AF.Ln, bias=EPS),
                       waits=_maxev(ev_var, ctx["prev_bc_r"]))
                ev_rstd = P.emit("scalar", lambda e: e.activation(rstd_row[:, :], var_row[:, :], AF.Exp, scale=-0.5),
                                 inc="ac")
                ctx["prev_rstd"] = ev_rstd
                ev_mr = P.emit("vector", lambda e: e.tensor_mul(mr_row[:, :], m_row[:, :], rstd_row[:, :]),
                               waits=_maxev(ev_rstd, ctx["prev_bc_mr"]), inc="dv")
                ctx["stats_ev"] = (ev_rstd, ev_mr)

            def bcast(tau):
                ev_rstd, ev_mr = ctx["stats_ev"]
                ev_bc_r = P.emit("tensor",
                                 lambda e: e.matmul(bank[2][:, :], ones1f[:, :], rstd_row[:, :], start=True, stop=True),
                                 waits=_maxev(ev_rstd, ev_bank_free[2]), inc="pe")
                ev_bc_mr = P.emit("tensor",
                                  lambda e: e.matmul(bank[3][:, :], ones1f[:, :], mr_row[:, :], start=True, stop=True),
                                  waits=_maxev(ev_mr, ev_bank_free[3]), inc="pe")
                ctx["prev_bc_r"] = ev_bc_r
                ctx["prev_bc_mr"] = ev_bc_mr
                ctx["bc_ev"] = (ev_bc_r, ev_bc_mr)

            def apply_ln(tau):
                xs = xs_r[tau % 2]
                y1 = y1_r[tau % 2]
                ev_bc_r, ev_bc_mr = ctx["bc_ev"]
                rb = bank[2][:, :].unsqueeze(1).broadcast_to([128, 16, TT])
                mb = bank[3][:, :].unsqueeze(1).broadcast_to([128, 16, TT])
                P.emit("vector",
                       lambda e, xs=xs, y1=y1, rb=rb: e.tensor_tensor(y1[:, :, :], xs[:, :, :], rb, ALU.mult),
                       waits=_maxev(ev_bc_r, ev_y1_free[tau % 2]))
                ev_ap = P.emit("vector",
                               lambda e, y1=y1, mb=mb: e.tensor_tensor(y1[:, :, :], y1[:, :, :], mb, ALU.subtract),
                               waits=_maxev(ev_bc_mr), inc="dv")
                ctx["ev_apply"][tau] = ev_ap
                ev_bank_free[2] = ev_ap
                ev_bank_free[3] = ev_ap
                ev_xs_free[tau % 2] = ev_ap

            def qk(tau):
                y1 = y1_r[tau % 2]
                ev_qk = None
                for c in range(16):
                    ev_qk = P.emit("tensor",
                                   lambda e, c=c, y1=y1: e.matmul(bank[0][:, :], wqk[:, c, :], y1[:, c, :],
                                                                  start=(c == 0), stop=(c == 15)),
                                   waits=_maxev(ctx["ev_apply"][tau],
                                                ev_bank_free[0]) if c == 0 else (),
                                   inc=("pe" if c == 15 else None))
                ev_kfc = P.emit("vector",
                                lambda e, tau=tau: e.tensor_copy(kf_aug[0:64, tau * TT:(tau + 1) * TT], bank[0][64:128, :]),
                                waits=_maxev(ev_qk), inc="dv")
                if tau < 2:
                    ev_kfc = P.emit("vector",
                                    lambda e, tau=tau: e.tensor_copy(qf_aug[0:64, tau * TT:(tau + 1) * TT], bank[0][0:64, :]),
                                    inc="dv")
                ev_bank_free[0] = ev_kfc
                ev_kf[tau] = ev_kfc

            def s_probe(jh):
                jt, hf, ic = jh
                ii = hf * TT + ic * 128
                ev_s = P.emit("tensor",
                       lambda e, ii=ii, jt=jt: e.matmul(
                           bank[7][:, :],
                           qf_aug[0:64, ii:ii + 128],
                           kf_aug[0:64, jt * TT:(jt + 1) * TT],
                           start=True, stop=True),
                       waits=_maxev(ev_kf[jt], ev_kf[1], ctx["ev_s_free"]), inc="pe")
                col = hf * 32 + ic * 8 + jt
                ctx["ev_s_free"] = P.emit("vector",
                               lambda e, col=col: e.reduce_max(
                                   mx_parts[:, col:col + 1], bank[7][:, :], axis=AX),
                               waits=_maxev(ev_s), inc="dv")

            def v_half(tau, half, s_work):
                y1 = y1_r[tau % 2]
                ev_pe_v = None
                for g in range(8):
                    ts, ob = divmod(half * 8 + g, 4)
                    k = (4 * ts + ob) % 3
                    pv = 4 + (4 * ts + ob) % 3
                    for c in range(16):
                        ev_pe_v = P.emit("tensor",
                               lambda e, pv=pv, c=c, ts=ts, ob=ob, y1=y1: e.matmul(
                                   bank[pv][:, :],
                                   y1[:, c, ts * 128:(ts + 1) * 128],
                                   wv_r[:, c, ob * TT:(ob + 1) * TT],
                                   start=(c == 0), stop=(c == 15)),
                               waits=_maxev(ctx["ev_apply"][tau],
                                            (ev_wv if (tau == 0 and ts == 0 and ob == 0) else None),
                                            ev_bank_free[pv]) if c == 0 else (),
                               inc=("pe" if c == 15 else None))
                    ev_ev = P.emit("scalar",
                                   lambda e, k=k, pv=pv: e.copy(v_sb[k][:, :], bank[pv][:, :]),
                                   waits=_maxev(ev_pe_v, ev_vsb_free[k]), inc="ac")
                    ev_bank_free[pv] = ev_ev
                    ev_st = P.dma("scalar", f"v{k}",
                                  v_dram[tau * TT + ts * 128: tau * TT + (ts + 1) * 128,
                                         ob * TT:(ob + 1) * TT],
                                  v_sb[k][:, :], waits=_maxev(ev_ev))
                    ev_vsb_free[k] = ev_st
                    while s_work:
                        s_probe(s_work.pop(0))
                        if len(s_work) % 1 == 0:
                            break
                if half == 1:
                    ev_y1_free[tau % 2] = ev_pe_v

            ctx["ev_s_free"] = None
            load_x(0)
            stats(0)
            bcast(0)
            apply_ln(0)
            load_x(1)
            for tau in range(NTT):
                qk(tau)
                if tau + 1 < NTT:
                    stats(tau + 1)
                if tau + 2 < NTT:
                    load_x(tau + 2)
                jts = [] if tau < 2 else ([tau - 2] if tau < 7 else [5, 6, 7])
                s_work = [(jt, hf, ic) for jt in jts for hf in range(2) for ic in range(4)]
                v_half(tau, 0, s_work)
                if tau + 1 < NTT:
                    bcast(tau + 1)
                    apply_ln(tau + 1)
                v_half(tau, 1, s_work)
                while s_work:
                    s_probe(s_work.pop(0))
            state["ev_smax"] = P.now("dv")

            state["ev_kf"] = ev_kf
            state["a_pe"] = P.now("pe")
            state["a_dv"] = P.now("dv")
            state["a_ac"] = P.now("ac")
            state["a_vst"] = [P.now("v0"), P.now("v1"), P.now("v2")]
            if debug:
                ev = P.emit("vector", lambda e: e.nop(), inc="dv")
                P.dma("gpsimd", "dbg", dbg["kf"][0:64, :], kf_aug[0:64, :], waits=[ev])
                P.dma("gpsimd", "dbg", dbg["qf"][0:64, :], qf_aug[0:64, :], waits=[ev])

        def phase_b():
            ev_kf_all = _maxev(state["ev_kf"], state["a_ac"], state["a_dv"])
            ev_bank_free = [None] * 8
            ev_vt_free = [None, None, None, None]
            ev_wo_free = [None, None, None]
            ev_xr_free = [None, None]
            ev_pt_free = None
            for hf in range(2):
                i0 = hf * TT
                b4v = bank[4][:, :].bitcast(BF16)
                ev_tr4 = [None] * 4
                for ic in range(4):
                    P.emit("vector",
                           lambda e, hf=hf, ic=ic: e.reduce_max(
                               m_col[:, :], mx_parts[:, hf * 32 + ic * 8:hf * 32 + (ic + 1) * 8], axis=AX),
                           waits=_maxev(state["ev_smax"] if (hf == 0 and ic == 0) else None))
                    ev_mb = P.emit("vector",
                                   lambda e, ic=ic: e.tensor_copy(m_bf[:, :], m_col[:, :]),
                                   waits=_maxev(ev_tr4[ic - 1] if ic > 0 else None),
                                   inc="dv")
                    ev_tr4[ic] = P.emit("tensor",
                                   lambda e, ic=ic, b4v=b4v: e.transpose(
                                       b4v[0:1, ic * 128:(ic + 1) * 128], m_bf[:, :], idt[:, :]),
                                   waits=_maxev(ev_mb, ev_bank_free[4] if ic == 0 else None,
                                                ev_kf_all if (hf == 0 and ic == 0) else None), inc="pe")
                    ev_bank_free[4] = ev_tr4[ic]
                ev_aug = P.emit("vector",
                                lambda e, i0=i0, b4v=b4v: e.tensor_scalar_mul(
                                    qf_aug[64:65, i0:i0 + TT], b4v[0:1, 0:512], -1.0),
                                waits=_maxev(*ev_tr4), inc="dv")
                ev_bank_free[4] = ev_aug
                ev_exp = [None] * 16
                for jp in range(16):
                    pp = jp % 3
                    ev_st2 = None
                    for half in range(2):
                        jc = 2 * jp + half
                        ev_st2 = P.emit("tensor",
                               lambda e, pp=pp, jc=jc, half=half, i0=i0: e.matmul(
                                   bank[2 * pp + half][:, :],
                                   kf_aug[0:65, jc * 128:(jc + 1) * 128],
                                   qf_aug[0:65, i0:i0 + TT],
                                   start=True, stop=True),
                               waits=_maxev(ev_aug if (jp == 0 and half == 0) else None,
                                            _maxev(ev_bank_free[2 * pp], ev_bank_free[2 * pp + 1])
                                            if half == 0 else None),
                               inc=("pe" if half == 1 else None))
                    ev_exp[jp] = P.emit("scalar",
                                        lambda e, pp=pp, jp=jp: e.activation(
                                            pt_sb[:, 2 * jp:2 * jp + 2, :], pair[pp][:, :], AF.Exp),
                                        waits=_maxev(ev_st2,
                                                     ev_pt_free if jp == 0 else None,
                                                     state["a_pe"] if (hf == 0 and jp == 0) else None),
                                        inc="ac")
                    ev_bank_free[2 * pp] = ev_exp[jp]
                    ev_bank_free[2 * pp + 1] = ev_exp[jp]
                ev_l = None
                for jc in range(32):
                    ev_l = P.emit("tensor",
                           lambda e, jc=jc: e.matmul(bank[6][0:1, :], ones_bf[:, :], pt_sb[:, jc, :],
                                                     start=(jc == 0), stop=(jc == 31)),
                           waits=_maxev(ev_exp[jc // 2], ev_bank_free[6] if jc == 0 else None),
                           inc=("pe" if jc == 31 else None))
                ev_linv = P.emit("vector", lambda e: e.reciprocal(linv_row[:, :], bank[6][0:1, :]),
                                 waits=_maxev(ev_l), inc="dv")
                ev_bank_free[6] = ev_linv
                ev_lbc = P.emit("tensor",
                                lambda e: e.matmul(bank[7][:, :], ones1f[:, :], linv_row[:, :], start=True, stop=True),
                                waits=_maxev(ev_linv, ev_bank_free[7]), inc="pe")
                ev_lsb = P.emit("vector", lambda e: e.tensor_copy(linv_sb[:, :], bank[7][:, :]),
                                waits=_maxev(ev_lbc), inc="dv")
                ev_bank_free[7] = ev_lsb
                if hf == 1 and "C" in phases:
                    c_half(0, ev_bank_free, ev_x2_half0)
                if debug:
                    for jc in range(32):
                        P.dma("gpsimd", "dbg", dbg["pt"][jc * 128:(jc + 1) * 128, i0:i0 + TT], pt_sb[:, jc, :],
                              waits=_maxev(ev_exp[jc // 2]))
                ev_oa_ev = [None] * 16
                for db in range(2):
                    ev_pvl = None
                    for jc in range(32):
                        k = jc % 4
                        ev_vl = P.dma("sync" if k % 2 == 0 else "scalar", f"vt{k}", vt_sb[k][:, :],
                                      v_dram[jc * 128:(jc + 1) * 128, db * 1024:(db + 1) * 1024],
                                      waits=_maxev(state["a_vst"] if (hf == 0 and db == 0 and jc < 4) else None,
                                                   state["a_pe"] if (hf == 0 and db == 0 and jc < 4) else None,
                                                   ev_vt_free[k]))
                        for dc in range(8):
                            ev_pvl = P.emit("tensor",
                                   lambda e, dc=dc, k=k, jc=jc: e.matmul(
                                       bank[dc][:, :],
                                       vt_sb[k][:, dc * 128:(dc + 1) * 128],
                                       pt_sb[:, jc, :],
                                       start=(jc == 0), stop=(jc == 31)),
                                   waits=_maxev(ev_vl if dc == 0 else None,
                                                ev_exp[jc // 2] if dc == 0 else None,
                                                ev_lsb if (jc == 0 and dc == 0) else None,
                                                ev_bank_free[dc] if jc == 0 else None),
                                   inc=("pe" if dc == 7 else None))
                        ev_vt_free[k] = ev_pvl
                    for dc in range(8):
                        ev_oa_ev[db * 8 + dc] = P.emit(
                            "vector",
                            lambda e, db=db, dc=dc: e.tensor_tensor(
                                oa_sb[:, db * 8 + dc, :], bank[dc][:, :], linv_sb[:, :], ALU.mult),
                            waits=_maxev(ev_pvl,
                                         state["a_pe"] if (hf == 0 and db == 0 and dc == 0) else None),
                            inc="dv")
                        ev_bank_free[dc] = ev_oa_ev[db * 8 + dc]
                ev_oa = _maxev(*ev_oa_ev)
                woT_src = woT.rearrange("(c p) o -> p c o", c=16, p=128)
                ev_womm = None
                for oB in range(2):
                    for dc2 in range(16):
                        k = (oB * 16 + dc2) % 3
                        ev_wol = P.dma("gpsimd", f"wo{k}", wo_sb[k][:, :],
                                       woT_src[:, dc2, oB * 1024:(oB + 1) * 1024],
                                       waits=_maxev(ev_wo_free[k],
                                                    state["a_pe"] if (hf == 0 and oB == 0 and dc2 < 3) else None))
                        for oc in range(8):
                            ev_womm = P.emit("tensor",
                                   lambda e, oc=oc, k=k, dc2=dc2: e.matmul(
                                       bank[oc][:, :],
                                       wo_sb[k][:, oc * 128:(oc + 1) * 128],
                                       oa_sb[:, dc2, :],
                                       start=(dc2 == 0), stop=(dc2 == 15)),
                                   waits=_maxev(ev_wol if oc == 0 else None,
                                                ev_oa_ev[dc2] if oc == 0 else None,
                                                ev_bank_free[oc] if dc2 == 0 else None),
                                   inc=("pe" if oc == 7 else None))
                        ev_wo_free[k] = ev_womm
                    for op2 in range(4):
                        oo = oB * 8 + 2 * op2
                        k2 = op2 % 2
                        ev_xl = P.dma("sync", f"xr{k2}", xr_sb[k2][:, :],
                                      xH[hf, :, oo * TT:(oo + 2) * TT],
                                      waits=_maxev(ev_xr_free[k2],
                                                   state["a_pe"] if (hf == 0 and oB == 0 and op2 < 2) else None))
                        ev_x2c = P.emit("vector",
                                        lambda e, oo=oo, k2=k2, op2=op2, i0=i0: e.tensor_tensor(
                                            x2[:, oo:oo + 2, i0:i0 + TT],
                                            xr_sb[k2][:, :].rearrange("p (j t) -> p j t", j=2),
                                            pair[op2][:, :].rearrange("p (j t) -> p j t", j=2),
                                            ALU.add),
                                        waits=_maxev(ev_xl, ev_womm), inc="dv")
                        ev_xr_free[k2] = ev_x2c
                        ev_bank_free[2 * op2] = ev_x2c
                        ev_bank_free[2 * op2 + 1] = ev_x2c
                if hf == 0:
                    ev_x2_half0 = P.now("dv")
                else:
                    state["b_wo_pe"] = P.now("pe")
                    if "C" in phases:
                        c_half(1, ev_bank_free, P.now("dv"))
                ev_pt_free = _maxev(P.now("pe"),
                                    P.now("dbg") if (debug and "dbg" in P.counts) else None)
            state["b_x2"] = P.now("dv")
            state["b_pe"] = P.now("pe")
            state["b_ac"] = P.now("ac")
            if debug:
                ev = state["b_x2"]
                for oo in range(16):
                    P.dma("gpsimd", "dbg", dbg["x2"][oo * 128:(oo + 1) * 128, :], x2[:, oo, :], waits=_maxev(ev))

        cctx = {"prev_bc": None, "prev_rstd": None, "y2_ev": [None, None]}

        def c_half(tt, ev_bank_free, ev_x2_half):
            ev_cast_pe = [None, None]
            ev_sq_pe = [None, None]
            ev_sq = [None] * 16
            ev_sums = None
            for c in range(16):
                ev_cast = P.emit("vector",
                                 lambda e, c=c, tt=tt: e.tensor_copy(cb2[c % 2][:, :], x2[:, c, tt * TT:(tt + 1) * TT]),
                                 waits=_maxev(ev_x2_half if c == 0 else None,
                                              state["a_vst"] if (tt == 0 and c == 0) else None,
                                              ev_cast_pe[c % 2], ev_sq[c - 2] if c >= 2 else None),
                                 inc="dv")
                ev_sq[c] = P.emit("scalar",
                                  lambda e, c=c: e.activation(xsq[c % 2][:, :], cb2[c % 2][:, :], AF.Square),
                                  waits=_maxev(ev_cast, ev_sq_pe[c % 2]),
                                  inc="ac")
                ev_cast_pe[c % 2] = P.emit(
                    "tensor",
                    lambda e, c=c: e.matmul(bank[0][0:1, :], ones_bf[:, :], cb2[c % 2][:, :],
                                            start=(c == 0), stop=(c == 15)),
                    waits=_maxev(ev_cast, ev_bank_free[0] if c == 0 else None), inc="pe")
                ev_sq_pe[c % 2] = P.emit(
                    "tensor",
                    lambda e, c=c: e.matmul(bank[1][0:1, :], ones_bf[:, :], xsq[c % 2][:, :],
                                            start=(c == 0), stop=(c == 15)),
                    waits=_maxev(ev_sq[c], ev_bank_free[1] if c == 0 else None), inc="pe")
                ev_sums = ev_sq_pe[c % 2]
            P.emit("vector", lambda e: e.tensor_scalar_mul(m_row[:, :], bank[0][0:1, :], 1.0 / D),
                   waits=_maxev(ev_sums))
            ev_m = P.emit("vector", lambda e: e.tensor_mul(var_row[:, :], m_row[:, :], m_row[:, :]),
                          waits=_maxev(cctx["prev_rstd"]), inc="dv")
            ev_bank_free[0] = ev_m
            ev_var = P.emit("vector",
                            lambda e: e.scalar_tensor_tensor(var_row[:, :], bank[1][0:1, :], 1.0 / D,
                                                             var_row[:, :], ALU.mult, ALU.subtract),
                            inc="dv")
            ev_bank_free[1] = ev_var
            P.emit("scalar", lambda e: e.activation(var_row[:, :], var_row[:, :], AF.Ln, bias=EPS),
                   waits=_maxev(ev_var, cctx["prev_bc"]))
            ev_rstd = P.emit("scalar", lambda e: e.activation(rstd_row[:, :], var_row[:, :], AF.Exp, scale=-0.5),
                             inc="ac")
            cctx["prev_rstd"] = ev_rstd
            ev_mr = P.emit("vector", lambda e: e.tensor_mul(mr_row[:, :], m_row[:, :], rstd_row[:, :]),
                           waits=_maxev(ev_rstd, cctx["prev_bc"]), inc="dv")
            ev_bc_r = P.emit("tensor",
                             lambda e: e.matmul(bank[2][:, :], ones1f[:, :], rstd_row[:, :], start=True, stop=True),
                             waits=_maxev(ev_rstd, ev_bank_free[2]), inc="pe")
            ev_bc_mr = P.emit("tensor",
                              lambda e: e.matmul(bank[3][:, :], ones1f[:, :], mr_row[:, :], start=True, stop=True),
                              waits=_maxev(ev_mr, ev_bank_free[3]), inc="pe")
            cctx["prev_bc"] = _maxev(ev_bc_r, ev_bc_mr)
            ev_cp_r = P.emit("vector", lambda e: e.tensor_copy(rb2_sb[:, :], bank[2][:, :]),
                             waits=_maxev(ev_bc_r), inc="dv")
            ev_cp_m = P.emit("vector", lambda e: e.tensor_copy(mb2_sb[:, :], bank[3][:, :]),
                             waits=_maxev(ev_bc_mr), inc="dv")
            ev_bank_free[2] = ev_cp_r
            ev_bank_free[3] = ev_cp_m
            rb = rb2_sb[:, :].unsqueeze(1).broadcast_to([128, 16, TT])
            mb = mb2_sb[:, :].unsqueeze(1).broadcast_to([128, 16, TT])
            y2v = y2h[:, :, :] if tt == 0 else y2_sb[:, :, TT:2 * TT]
            P.emit("vector",
                   lambda e, tt=tt, rb=rb, y2v=y2v: e.tensor_tensor(
                       y2v, x2[:, :, tt * TT:(tt + 1) * TT], rb, ALU.mult),
                   waits=_maxev((P.now("dbg") if (debug and "dbg" in P.counts) else None)
                                if tt == 1 else None))
            last = P.emit("vector",
                          lambda e, y2v=y2v, mb=mb: e.tensor_tensor(y2v, y2v, mb, ALU.subtract),
                          inc="dv")
            cctx["y2_ev"][tt] = last

        def phase_c():
            state["c_y2_tt"] = [cctx["y2_ev"][0], cctx["y2_ev"][1]]
            state["c_y2"] = _maxev(cctx["y2_ev"][0], cctx["y2_ev"][1])
            state["c_pe"] = P.now("pe")
            if debug:
                ev = state["c_y2"]
                for c in range(16):
                    P.dma("gpsimd", "dbg", dbg["y2"][c * 128:(c + 1) * 128, :], y2_sb[:, c, :], waits=_maxev(ev))

        def phase_d():
            ev_bank_free = [None] * 8
            ev_w_free = [None, None]
            ev_sg_free = [None] * 4
            ev_h1_free = [None, None]
            ev_y2 = state["c_y2"]
            pend = []

            def emit_h128(a_prev, ev_hids):
                kb = a_prev % 2
                ks0 = 2 * (a_prev % 2)
                ev_ev = [None, None]
                for tt in range(2):
                    bh = 4 + 2 * kb + tt
                    ev_h128 = P.emit("tensor",
                           lambda e, bh=bh, ks=ks0 + tt: e.matmul(bank[bh][:, :], h128[:, :], sg_sb[ks][:, :],
                                                                  start=True, stop=True),
                           waits=_maxev(ev_hids[tt], ev_bank_free[bh]), inc="pe")
                    ev_sg_free[ks0 + tt] = ev_h128
                    ev_ev[tt] = P.emit("scalar",
                                       lambda e, bh=bh, kb=kb, tt=tt: e.copy(
                                           h1_sb[kb][:, tt * TT:(tt + 1) * TT], bank[bh][:, :]),
                                       waits=_maxev(ev_h128, ev_h1_free[kb] if tt == 0 else None), inc="ac")
                    ev_bank_free[bh] = ev_ev[tt]
                h1_dst = h1_dram.rearrange("(b s a) t -> s b (a t)", b=64, s=2, a=64)
                ev_h1st = P.dma("sync", f"h1_{kb}",
                                h1_dst[:, :, a_prev * OWN:(a_prev + 1) * OWN],
                                h1_sb[kb][:, :], waits=_maxev(ev_ev[0], ev_ev[1]))
                ev_h1_free[kb] = ev_h1st

            for a2 in range(32):
                k = a2 % 2
                ev_w1 = P.dma("gpsimd", f"w1_{k}", w1_sb[k][:, :, :].rearrange("p c t -> p (c t)"),
                              w1H[a2, :, :],
                              waits=_maxev(ev_w_free[k],
                                           state["b_wo_pe"] if a2 < 2 else None,
                                           (P.now("dbg") if (debug and "dbg" in P.counts) else None)
                                           if a2 < 2 else None,
                                           [P.now("h1_0"), P.now("h1_1")] if a2 < 2 else None))
                ev_w3 = P.dma("gpsimd", f"w3_{k}", w3_sb[k][:, :, :].rearrange("p c t -> p (c t)"),
                              w3H[a2, :, :],
                              waits=_maxev(state["b_wo_pe"] if a2 < 2 else None,
                                           state["b_x2"] if a2 < 2 else None))
                for j in range(2):
                    a = 2 * a2 + j
                    ks0 = 2 * (a % 2)
                    order = ([(c, tt) for tt in range(2) for c in range(16)]
                             if a == 0 else
                             [(c, tt) for c in range(16) for tt in range(2)])
                    def y2rhs(c, tt):
                        return y2h[:, c, :] if tt == 0 else y2_sb[:, c, TT:2 * TT]
                    ev_g = None
                    for c, tt in order:
                        ev_g = P.emit("tensor",
                               lambda e, c=c, k=k, j=j, tt=tt: e.matmul(
                                   bank[tt][:, :], w1_sb[k][:, c, j * 128:(j + 1) * 128],
                                   y2rhs(c, tt),
                                   start=(c == 0), stop=(c == 15)),
                               waits=_maxev(ev_w1 if (c == 0 and tt == 0) else None,
                                            state["c_y2_tt"][tt] if (a == 0 and c == 0) else None,
                                            _maxev(ev_bank_free[0], ev_bank_free[1])
                                            if (c == 0 and tt == 0) else None),
                               inc=("pe" if (c == 15 and tt == 1) else None))
                    if pend:
                        emit_h128(*pend.pop())
                    ev_lm = None
                    for c, tt in order:
                        ev_lm = P.emit("tensor",
                               lambda e, c=c, k=k, j=j, tt=tt: e.matmul(
                                   bank[2 + tt][:, :], w3_sb[k][:, c, j * 128:(j + 1) * 128],
                                   y2rhs(c, tt),
                                   start=(c == 0), stop=(c == 15)),
                               waits=_maxev(ev_w3 if (c == 0 and tt == 0) else None,
                                            _maxev(ev_bank_free[2], ev_bank_free[3])
                                            if (c == 0 and tt == 0) else None),
                               inc=("pe" if (c == 15 and tt == 1) else None))
                    ev_hids = [None, None]
                    for tt in range(2):
                        ev_silu = P.emit("scalar",
                                         lambda e, tt=tt, ks=ks0 + tt: e.activation(sg_sb[ks][:, :], bank[tt][:, :], AF.Silu),
                                         waits=_maxev(ev_g, ev_sg_free[ks0 + tt]), inc="ac")
                        if tt == 1:
                            ev_bank_free[0] = ev_silu
                            ev_bank_free[1] = ev_silu
                        ev_hids[tt] = P.emit("vector",
                                             lambda e, tt=tt, ks=ks0 + tt: e.tensor_tensor(
                                                 sg_sb[ks][:, :], sg_sb[ks][:, :], bank[2 + tt][:, :], ALU.mult),
                                             waits=_maxev(ev_silu, ev_lm), inc="dv")
                        if tt == 1:
                            ev_bank_free[2] = ev_hids[1]
                            ev_bank_free[3] = ev_hids[1]
                    pend.append((a, ev_hids))
                ev_w_free[k] = _maxev(P.now("pe"))
            emit_h128(*pend.pop())
            state["d_h1st"] = [P.now("h1_0"), P.now("h1_1")]
            state["d_ac"] = P.now("ac")
            state["d_dv"] = P.now("dv")
            state["d_pe"] = P.now("pe")

        def phase_e():
            h1_src = h1_dram.rearrange("(b p) t -> p b t", p=128)
            h2_src = h2_dram.rearrange("(b p) t -> p b t", p=128)
            ev_hp_free = [None, None, None]
            ev_h2_free = [None, None, None, None]
            ev_bank_free = [None] * 8
            ev_h2st = [None] * 32
            for bp in range(32):
                k = bp % 3
                ev_l = P.dma("sync", f"hp{k}", hp_sb[k][:, :],
                             h1_src[:, 2 * bp:2 * bp + 2, :],
                             waits=_maxev(state["d_h1st"] if bp < 3 else None,
                                          state["d_pe"] if bp < 3 else None,
                                          ev_hp_free[k]))
                for bj in range(2):
                    b = 2 * bp + bj
                    kb = b % 4
                    pp = b % 4
                    ev_mm = None
                    for half in range(2):
                        ev_mm = P.emit("tensor",
                               lambda e, pp=pp, half=half, k=k, bj=bj: e.matmul(
                                   bank[2 * pp + half][:, :], h64p[:, :],
                                   hp_sb[k][:, bj * 1024 + half * TT: bj * 1024 + (half + 1) * TT],
                                   start=True, stop=True),
                               waits=_maxev(ev_l if half == 0 else None,
                                            _maxev(ev_bank_free[2 * pp], ev_bank_free[2 * pp + 1])
                                            if half == 0 else None,
                                            _maxev(state["d_ac"], state["d_dv"]) if b < 4 else None),
                               inc=("pe" if half == 1 else None))
                    if bj == 1:
                        ev_hp_free[k] = ev_mm
                    ev_ev = P.emit("scalar",
                                   lambda e, pp=pp, kb=kb: e.activation(
                                       h2_sb[kb][:, :], pair[pp][:, :], AF.Copy, scale=FWHT_SCALE),
                                   waits=_maxev(ev_mm, ev_h2_free[kb]), inc="ac")
                    ev_bank_free[2 * pp] = ev_ev
                    ev_bank_free[2 * pp + 1] = ev_ev
                    ev_st = P.dma("scalar", f"h2_{kb}",
                                  h2_dram[b * 128:(b + 1) * 128, :], h2_sb[kb][:, :],
                                  waits=_maxev(ev_ev))
                    if bj == 0:
                        ev_h2st[bp] = [ev_st]
                    else:
                        ev_h2st[bp] = ev_h2st[bp] + [ev_st]
                    ev_h2_free[kb] = ev_st
            ev_h2st_all = [P.now(f"h2_{i}") for i in range(4)]
            ev_w2_free = [None, None, None]
            ev_x3_free = [None, None]
            for oB in range(4):
                for bp in range(32):
                    k = bp % 3
                    ev_wl = P.dma("gpsimd", f"w2_{k}", w2_sb[k][:, :],
                                  w2H[oB, bp, :, :],
                                  waits=_maxev(ev_w2_free[k],
                                               state["d_pe"] if (oB == 0 and bp < 3) else None,
                                               state["b_x2"] if (oB == 0 and bp < 3) else None))
                    ev_hl = P.dma("sync", f"hp{k}", hp_sb[k][:, :],
                                  h2_src[:, 2 * bp:2 * bp + 2, :],
                                  waits=_maxev(ev_h2st[bp] if oB == 0 else None,
                                               ev_h2st_all if (oB == 0 and bp < 3) else None,
                                               ev_hp_free[k]))
                    evp = None
                    for bj in range(2):
                        for oc in range(4):
                            for tt in range(2):
                                bk = oc * 2 + tt
                                evp = P.emit("tensor",
                                       lambda e, bk=bk, oc=oc, tt=tt, k=k, bj=bj, bp=bp: e.matmul(
                                           bank[bk][:, :],
                                           w2_sb[k][:, bj * 512 + oc * 128: bj * 512 + (oc + 1) * 128],
                                           hp_sb[k][:, bj * 1024 + tt * TT: bj * 1024 + (tt + 1) * TT],
                                           start=(bp == 0 and bj == 0), stop=(bp == 31 and bj == 1)),
                                       waits=_maxev(ev_wl if (bj == 0 and oc == 0 and tt == 0) else None,
                                                    ev_hl if (bj == 0 and oc == 0 and tt == 0) else None,
                                                    ev_bank_free[bk]
                                                    if (bp == 0 and bj == 0) else None),
                                       inc=("pe" if (bj == 1 and tt == 1 and (bp == 31 or oc == 3)) else None))
                    if bp == 31:
                        for oc2 in range(4):
                            state[f"e2_stop{oc2}"] = ("pe", P.counts["pe"] - (3 - oc2))
                    ev_w2_free[k] = evp
                    ev_hp_free[k] = evp
                for oc in range(4):
                    oo = oB * 4 + oc
                    k3 = oc % 2
                    ev_add = P.emit("vector",
                                    lambda e, oc=oc, oo=oo, k3=k3: e.tensor_tensor(
                                        x3_sb[k3][:, :], x2[:, oo, :], pair[oc][:, :], ALU.add),
                                    waits=_maxev(state[f"e2_stop{oc}"], ev_x3_free[k3],
                                                 state["b_x2"] if (oB == 0 and oc == 0) else None),
                                    inc="dv")
                    ev_bank_free[2 * oc] = ev_add
                    ev_bank_free[2 * oc + 1] = ev_add
                    ev_out = P.dma("scalar", f"x3_{k3}",
                                   outT[oo * 128:(oo + 1) * 128, :],
                                   x3_sb[k3][:, :], waits=_maxev(ev_add))
                    ev_x3_free[k3] = ev_out
            state["out_done"] = [P.now("x3_0"), P.now("x3_1")]

        if "A" in phases:
            phase_a()
        if "B" in phases:
            phase_b()
        if "C" in phases:
            phase_c()
        if "D" in phases:
            phase_d()
        if "E" in phases:
            phase_e()

        if "E" in phases:
            P.emit("sync", lambda e: e.nop(),
                   waits=_maxev(state["out_done"],
                                (P.now("dbg") if "dbg" in P.counts else None) if debug else None))
        else:
            ev = P.emit("gpsimd", lambda e: e.memset(x3_sb[0][:, :], 0.0),
                        waits=_maxev(state.get("b_x2"), state.get("c_y2"),
                                     state.get("a_vst"), P.now("dv"), P.now("ac"),
                                     P.now("pe")),
                        inc="gp")
            for oo in range(16):
                P.dma("sync", "ph", outT[oo * 128:(oo + 1) * 128, :],
                      x3_sb[0][:, :], waits=_maxev(ev))
            P.emit("sync", lambda e: e.nop(),
                   waits=_maxev(P.now("ph"),
                                (P.now("dbg") if "dbg" in P.counts else None) if debug else None))

        with nc.Block() as block:
            P.finalize_block(block)

    return nc



def _hadamard(n):
    h = np.array([[1.0]], dtype=np.float32)
    while h.shape[0] < n:
        h = np.block([[h, h], [h, -h]])
    return h


def prep_inputs(L0_attn_hidden_states, L0_attn_Wq, L0_attn_Wk, L0_attn_Wv,
                L0_attn_Wo, L0_ffn_W1, L0_ffn_W2, L0_ffn_W3):
    x = np.asarray(L0_attn_hidden_states)
    w1T = np.asarray(L0_ffn_W1).T.astype(np.float32)
    w3T = np.asarray(L0_ffn_W3).T.astype(np.float32)
    w2T = np.asarray(L0_ffn_W2).T.astype(np.float32)
    wvT = np.asarray(L0_attn_Wv).T.astype(np.float32)
    base = {
        "wqkT": np.ascontiguousarray(
            np.concatenate([np.asarray(L0_attn_Wq), np.asarray(L0_attn_Wk)], axis=0).T).astype(np.float32),
        "woT": np.ascontiguousarray(np.asarray(L0_attn_Wo).T).astype(np.float32),
        "wvH": np.ascontiguousarray(
            wvT.reshape(16, 128, D).transpose(1, 0, 2).reshape(128, 16 * D)),
        "w1H": np.ascontiguousarray(
            w1T.reshape(16, 128, 32, 256).transpose(2, 1, 0, 3).reshape(32, 128, 16 * 256)),
        "w3H": np.ascontiguousarray(
            w3T.reshape(16, 128, 32, 256).transpose(2, 1, 0, 3).reshape(32, 128, 16 * 256)),
        "w2H": np.ascontiguousarray(
            w2T.reshape(64, 2, 32, 2, 4, 512).transpose(4, 2, 1, 0, 3, 5).reshape(4, 32, 128, 1024)),
        "idt": np.eye(128, dtype=ml_dtypes.bfloat16),
        "h128": _hadamard(128).astype(ml_dtypes.bfloat16),
    }
    h64 = _hadamard(64)
    h64p = np.zeros((128, 128), dtype=np.float32)
    h64p[0:64, 0:64] = h64
    h64p[64:128, 64:128] = h64
    base["h64p"] = h64p.astype(ml_dtypes.bfloat16)
    in_maps = []
    for core in range(8):
        b, r = core // 4, core % 4
        m = dict(base)
        xb = np.roll(x[b].T, -r * OWN, axis=1).astype(np.float32)
        m["xH"] = np.ascontiguousarray(
            xb.reshape(16, 128, 8, 512).transpose(2, 1, 0, 3).reshape(8, 128, 16 * 512))
        in_maps.append(m)
    return in_maps


def kernel(**inputs):
    in_maps = prep_inputs(**inputs)
    nc = build_program(debug=False)
    res = run_bass_kernel_spmd(nc, in_maps, core_ids=list(range(8)))
    out = np.empty((2, S, D), dtype=np.float32)
    for core in range(8):
        b, r = core // 4, core % 4
        out[b, r * OWN:(r + 1) * OWN, :] = res.results[core]["outT"].T
    return out
